# version 03
# baseline (speedup 1.0000x reference)
import contextlib
import numpy as np
import ml_dtypes
import concourse.bass as bass
import concourse.mybir as mybir
from concourse.bass_utils import run_bass_kernel_spmd

F32 = mybir.dt.float32
BF16 = mybir.dt.bfloat16
AF = mybir.ActivationFunctionType
ALU = mybir.AluOpType

B, S, D, H, HD, FF = 4, 4096, 1024, 16, 64, 4096


class Ctr:
    __slots__ = ("sem", "step", "n")

    def __init__(self, sem, step):
        self.sem, self.step, self.n = sem, step, 0


def build_nc(debug=False, upto="C", a1mode="full", serial=False, dumps=None):
    if dumps is None:
        dumps = debug
    import os as _os0
    _DS = set(_os0.environ.get("DUMPSET", "a,b,c1,c2,c3").split(","))
    nc = bass.Bass("TRN2", target_bir_lowering=False)
    SCRATCH_KIND = "ExternalOutput" if debug else "Internal"

    d_x = nc.dram_tensor("x", [S, D], F32, kind="ExternalInput").ap()
    d_xq = nc.dram_tensor("xq", [2048, D], F32, kind="ExternalInput").ap()
    d_wqT = nc.dram_tensor("wqT", [D, D], F32, kind="ExternalInput").ap()
    d_wkT = nc.dram_tensor("wkT", [D, D], F32, kind="ExternalInput").ap()
    d_wvT = nc.dram_tensor("wvT", [D, D], F32, kind="ExternalInput").ap()
    d_woT = nc.dram_tensor("woT", [D, D], F32, kind="ExternalInput").ap()
    d_w1T = nc.dram_tensor("w1T", [D, FF], F32, kind="ExternalInput").ap()
    d_w2T = nc.dram_tensor("w2T", [FF, D], F32, kind="ExternalInput").ap()
    d_b1 = nc.dram_tensor("b1", [128, 32], F32, kind="ExternalInput").ap()
    d_b2 = nc.dram_tensor("b2", [128, 8], F32, kind="ExternalInput").ap()
    d_bo = nc.dram_tensor("bo", [D], F32, kind="ExternalInput").ap()
    d_eye = nc.dram_tensor("eye", [128, 128], BF16, kind="ExternalInput").ap()
    d_eyef = nc.dram_tensor("eyef", [128, 128], F32, kind="ExternalInput").ap()
    d_md1 = nc.dram_tensor("md1", [128, 128], F32, kind="ExternalInput").ap()
    d_md2 = nc.dram_tensor("md2", [128, 128], F32, kind="ExternalInput").ap()
    d_y = nc.dram_tensor("y", [2048, D], F32, kind="ExternalOutput").ap()
    d_kS = nc.dram_tensor("kS", [8, 128, S], BF16, kind=SCRATCH_KIND).ap()
    d_vS = nc.dram_tensor("vS", [S, D], BF16, kind=SCRATCH_KIND).ap()
    d_qS = nc.dram_tensor("qS", [8, 128, 2048], BF16, kind=SCRATCH_KIND).ap()
    d_x2S = nc.dram_tensor("x2S", [2048, D], F32, kind=SCRATCH_KIND).ap()
    d_w2qS = nc.dram_tensor("w2qS", [FF, D], BF16, kind=SCRATCH_KIND).ap()
    d_w1bS = nc.dram_tensor("w1bS", [D, FF], BF16, kind=SCRATCH_KIND).ap()
    if debug:
        d_lnS = nc.dram_tensor("lnS", [32, 128, D], BF16, kind="ExternalOutput").ap()
        d_xlnS = nc.dram_tensor("xlnS", [8, 128, 8 * 512], BF16, kind="ExternalOutput").ap()
        d_afmS = nc.dram_tensor("afmS", [4, 128, 4096], BF16, kind="ExternalOutput").ap()
        d_epiS = nc.dram_tensor("epiS", [16, 3, 128, D], F32, kind="ExternalOutput").ap()
        d_sbbS = nc.dram_tensor("sbbS", [8, 128, 512], F32, kind="ExternalOutput").ap()
        d_numS = nc.dram_tensor("numS", [8, 128, 512], F32, kind="ExternalOutput").ap()
        d_eS = nc.dram_tensor("eS", [8, 4, 128, 512], BF16, kind="ExternalOutput").ap()
        d_xln2S = nc.dram_tensor("xln2S", [8, 128, 2048], BF16, kind="ExternalOutput").ap()
        d_x2cS = nc.dram_tensor("x2cS", [8, 128, 2048], F32, kind="ExternalOutput").ap()
        d_lnbS = nc.dram_tensor("lnbS", [8, 2, 128, 1024], BF16, kind="ExternalOutput").ap()
        d_hqS = nc.dram_tensor("hqS", [8, 8, 128, 1024], BF16, kind="ExternalOutput").ap()
        d_ofmS = nc.dram_tensor("ofmS", [8, 2, 128, 2048], F32, kind="ExternalOutput").ap()
        d_otmS = nc.dram_tensor("otmS", [8, 128, 2048], F32, kind="ExternalOutput").ap()

    es = contextlib.ExitStack()
    with es:
        sbuf = lambda n, sh, dt: es.enter_context(nc.sbuf_tensor(n, sh, dt))
        psumt = lambda n, sh, dt: es.enter_context(nc.psum_tensor(n, sh, dt))
        sem = lambda n: es.enter_context(nc.semaphore(n))

        AR16 = sbuf("AR16", [128, 68096], BF16)
        AR32 = sbuf("AR32", [128, 12544], F32)
        s_eye = sbuf("s_eye", [128, 128], BF16)
        s_eyef = sbuf("s_eyef", [128, 128], F32)
        s_md1 = sbuf("s_md1", [128, 128], F32)
        s_md2 = sbuf("s_md2", [128, 128], F32)
        s_b1 = sbuf("s_b1", [128, 32], F32)
        s_b2 = sbuf("s_b2", [128, 8], F32)
        s_b2q = sbuf("s_b2q", [128, 8], F32)
        s_bo = sbuf("s_bo", [128, 1024], F32)
        s_stat = sbuf("s_stat", [128, 12], F32)
        s_mv = sbuf("s_mv", [128, 2], F32)
        s_rstd = sbuf("s_rstd", [128, 1], F32)
        s_ln4 = sbuf("s_ln4", [128, 4], F32)
        s_junk = sbuf("s_junk", [128, 2048], F32)
        s_K = sbuf("s_K", [128, 8], F32)
        s_ones = sbuf("s_ones", [128, 128], BF16)
        s_bor = sbuf("s_bor", [1, 1024], BF16)
        s_rS = sbuf("s_rS", [128, 512], BF16)
        s_Bb = sbuf("s_Bb", [128, 512], F32)
        s_cst = sbuf("s_cst", [128, 2], F32)

        P = [psumt(f"P{i}", [128, 512], F32) for i in range(7)]
        PT = psumt("PT", [128, 1024], BF16)

        cT = Ctr(sem("smT"), 1)
        cV = Ctr(sem("smV"), 1)
        cS = Ctr(sem("smS"), 1)
        cG = Ctr(sem("smG"), 1)
        ENG_CTR = {"tensor": cT, "vector": cV, "scalar": cS, "gpsimd": cG}
        cDI = Ctr(sem("sdI"), 16)
        cGI = Ctr(sem("sgI"), 16)
        cDS = Ctr(sem("sdS"), 16)
        cGS = Ctr(sem("sgS"), 16)
        SD = [Ctr(sem(f"sd{i}"), 16) for i in range(10)]
        GDp = [Ctr(sem(f"sg{i}"), 16) for i in range(2)]
        ALL_CTRS = [cT, cV, cS, cG, cDI, cGI, cDS, cGS] + SD + GDp

        order = []
        _prev = [None]

        def em(eng, fn, waits=(), inc=None):
            if serial and fn is not None and inc is None:
                inc = ENG_CTR.get(eng) or (cD if eng == "sync" else cG)
            tok = None
            if inc is not None:
                inc.n += 1
                tok = (inc, inc.n)
            waits = tuple(w for w in waits if w is not None)
            if serial and fn is not None:
                waits = waits + ((_prev[0],) if _prev[0] else ())
                _prev[0] = tok
            order.append((eng, fn, waits, inc))
            return tok

        def emt(eng, fn, waits=()):
            return em(eng, fn, waits, inc=ENG_CTR[eng])

        def dma(eng, ctr, out_ap, in_ap, waits=(), slow=False):
            return em(eng, lambda e, o=out_ap, i=in_ap, s=slow: e.dma_start(
                o, i, allow_slow_non_contiguous=s), waits, inc=ctr)

        def barrier():
            snap = [(c, c.n) for c in ALL_CTRS if c.n > 0]
            for eng in ("sync", "tensor", "vector", "scalar", "gpsimd"):
                em(eng, None, waits=snap)

        WO_O, WQ_O, WK_O, WV_O, SCR_O, E_O = 0, 8192, 16384, 24576, 32768, 65536
        A_XLN, A_KST, A_VST, A_QST = SCR_O, SCR_O + 8192, SCR_O + 16384, SCR_O + 18432
        A_LNBF = E_O
        C_W2Q, C_W1G, C_HQ, C_XLN2 = SCR_O, WQ_O, WK_O, WV_O
        B_KC, B_QB, B_AFM, B_VAL, B_E = WK_O, WV_O, WV_O + 4096, SCR_O, E_O
        FA0 = 0
        qt = [8192 + i * 1024 for i in range(4)]

        def a16(off, n):
            assert off + n <= 68096, (off, n)
            return AR16[:, off:off + n]

        def a32(off, n):
            assert off + n <= 12544, (off, n)
            return AR32[:, off:off + n]

        nc.const_aps.register(F32, 1e-5, s_cst[:, 0:1])
        em("vector", lambda e: e.memset(s_cst[:, 0:1], 1e-5))
        em("vector", lambda e: e.memset(s_ones[:], 1.0))
        kcol = lambda j: s_K[:, j:j + 1]
        tt0 = emt("vector", lambda e: e.memset(kcol(0), float(2 ** 25)))
        tt0 = emt("scalar", lambda e: e.copy(kcol(6), kcol(0)), [tt0])
        tK = emt("vector", lambda e: e.tensor_scalar(kcol(1), kcol(6), 1.0, kcol(6),
                                                     ALU.add, ALU.subtract), [tt0])
        tK = emt("scalar", lambda e: e.copy(kcol(6), kcol(1)), [tK])

        def ceff(j, Ash, Bnat, tprev):
            v1 = emt("vector", lambda e, A=Ash: e.tensor_scalar(kcol(7), kcol(1), float(A),
                                                                None, ALU.mult), [tprev, tK])
            s1 = emt("scalar", lambda e: e.copy(kcol(6), kcol(7)), [v1])
            v2 = emt("vector", lambda e, Bn=Bnat, j=j: e.tensor_scalar(
                kcol(j), kcol(1), float(-Bn), float(Bn), ALU.mult, ALU.add), [s1])
            s2 = emt("scalar", lambda e, j=j: e.copy(kcol(6), kcol(j)), [v2])
            v3 = emt("vector", lambda e, j=j: e.tensor_tensor(kcol(j), kcol(j), kcol(7),
                                                              ALU.add), [s2])
            return emt("scalar", lambda e, j=j: e.copy(kcol(6), kcol(j)), [v3])

        t2_ = ceff(2, 2.0 ** 49, float(2 ** 20 + 1), tK)
        t3_ = ceff(3, 3.0 * 2 ** 51, float(3 * 2 ** 22), t2_)
        t4_ = ceff(4, 2.0 ** 42, float(2 ** 13 + 1), t3_)
        setup_tok = emt("scalar", lambda e: e.copy(kcol(5), kcol(4)), [t4_])

        dma("sync", cDI, s_eye[:], d_eye)
        dma("sync", cDI, s_eyef[:], d_eyef)
        dma("sync", cDI, s_md1[:], d_md1)
        dma("sync", cDI, s_md2[:], d_md2)
        dma("sync", cDI, s_b1[:], d_b1)
        dma("sync", cDI, s_b2[:], d_b2)
        init_ld = dma("sync", cDI, s_bo[0:1, :], d_bo.rearrange("(o f) -> o f", o=1))
        t_boc = emt("scalar", lambda e: e.copy(s_bor[:], s_bo[0:1, :]), waits=[init_ld])

        tbm1 = emt("tensor", lambda e: e.matmul(P[0][:], s_ones[0:1, :], s_bor[0:1, 0:512],
                                                start=True, stop=True), waits=[t_boc, setup_tok])
        tbm2 = emt("tensor", lambda e: e.matmul(P[1][:], s_ones[0:1, :], s_bor[0:1, 512:1024],
                                                start=True, stop=True), waits=[])
        tbe1 = emt("scalar", lambda e: e.copy(s_bo[:, 0:512], P[0][:]), waits=[tbm1])
        t_bo = emt("scalar", lambda e: e.copy(s_bo[:, 512:1024], P[1][:]), waits=[tbm2])
        bank_guard_init = {"P0": tbe1, "P1": t_bo}
        init_done = t_bo
        wload = None
        _wst_g = [None, None]
        _wlist = [] if a1mode == "ln" else [(WO_O, d_woT), (WK_O, d_wkT), (WV_O, d_wvT), (WQ_O, d_wqT)]
        for wi, (wofs, dw) in enumerate(_wlist):
            for qtr in range(4):
                pw = (wi * 4 + qtr) % 2
                stg = a32(8192 + pw * 2048, 2048)
                tld = dma("sync", SD[8 + pw], stg.rearrange("p (c f) -> p c f", c=2),
                          dw.rearrange("(c p) f -> p c f", p=128)[:, qtr * 2:qtr * 2 + 2, :],
                          waits=[_wst_g[pw]])
                wload = emt("scalar", lambda e, wofs=wofs, qtr=qtr, stg=stg: e.copy(
                    a16(wofs + qtr * 2048, 2048), stg), waits=[tld])
                _wst_g[pw] = wload

        g_lnj = [None]

        def emit_ln(src_ap, dst_ap, waits):
            t1 = emt("vector", lambda e: e.tensor_reduce(s_ln4[:, 0:1], src_ap,
                                                         mybir.AxisListType.X, ALU.add), waits)
            t2 = emt("scalar", lambda e: e.activation(s_junk[:, 1024:2048], src_ap, AF.Square),
                     list(waits) + [g_lnj[0]])
            t2 = emt("vector", lambda e: e.tensor_reduce(s_ln4[:, 1:2], s_junk[:, 1024:2048],
                                                         mybir.AxisListType.X, ALU.add), [t2])
            g_lnj[0] = t2
            tm = emt("scalar", lambda e: e.activation(s_mv[:, 0:1], s_ln4[:, 0:1], AF.Copy,
                                                      scale=1.0 / 1024.0), [t1])
            tmm = emt("vector", lambda e: e.tensor_tensor(s_ln4[:, 2:3], s_mv[:, 0:1],
                                                          s_mv[:, 0:1], ALU.mult), [tm])
            tbb = emt("scalar", lambda e: e.copy(s_ln4[:, 3:4], s_ln4[:, 2:3]), [tmm])
            tv = emt("vector", lambda e: e.scalar_tensor_tensor(s_mv[:, 1:2], s_ln4[:, 1:2],
                                                                1.0 / 1024.0, s_ln4[:, 2:3],
                                                                ALU.mult, ALU.subtract), [tbb, t2])
            t3 = emt("scalar", lambda e: e.activation(s_rstd[:], s_mv[:, 1:2], AF.Sqrt, bias=1e-5),
                     waits=[tv])
            t4 = emt("vector", lambda e: e.reciprocal(s_rstd[:], s_rstd[:]), waits=[t3])
            t5 = emt("scalar", lambda e: e.copy(s_ln4[:, 3:4], s_rstd[:]), [t4])
            return emt("vector", lambda e: e.tensor_scalar(dst_ap, src_ap, s_mv[:, 0:1],
                                                           s_rstd[:, 0:1], ALU.subtract, ALU.mult),
                       waits=[t5])

        g_qt = [None]

        def emit_quant(x_ap, out_ap, man, waits, tA, tB, tC, tD, bounce=False):
            Cm = kcol(2) if man == 3 else kcol(5)
            Cab = kcol(3)
            minexp = -6 if man == 3 else -14
            invg = float(2.0 ** (man - minexp))
            g = float(2.0 ** (minexp - man))
            thr = float(2.0 ** (2 * minexp))
            w = list(waits) + [setup_tok, g_qt[0]]
            t_w = emt("gpsimd", lambda e: e.tensor_scalar(tA, x_ap, Cm, None, ALU.mult), w)
            t_z = emt("vector", lambda e: e.tensor_tensor(tB, tA, x_ap, ALU.subtract), [t_w])
            if bounce:
                t_z = emt("scalar", lambda e: e.copy(s_ln4[:, 3:4], tB[:, 0:1]), [t_z])
            t_v = emt("vector", lambda e: e.tensor_tensor(tA, tA, tB, ALU.subtract), [t_z])
            t_a = emt("gpsimd", lambda e: e.tensor_scalar(tC, x_ap, invg, None, ALU.mult), w)
            t_a = emt("gpsimd", lambda e: e.tensor_scalar(tC, tC, Cab, Cab, ALU.add, ALU.subtract), [t_a])
            t_a = emt("gpsimd", lambda e: e.tensor_scalar(tC, tC, g, None, ALU.mult), [t_a])
            t_m = emt("scalar", lambda e: e.activation(tD, x_ap, AF.Square), w)
            t_m = emt("gpsimd", lambda e: e.tensor_scalar(tD, tD, thr, None, ALU.is_ge), [t_m])
            if bounce:
                t_v = emt("scalar", lambda e: e.copy(s_ln4[:, 3:4], tA[:, 0:1]), [t_v])
            t_b = emt("vector", lambda e: e.tensor_tensor(tA, tA, tC, ALU.subtract), [t_v, t_a])
            if bounce:
                t_b = emt("scalar", lambda e: e.copy(s_ln4[:, 3:4], tA[:, 0:1]), [t_b])
            t_b = emt("vector", lambda e: e.tensor_tensor(tA, tA, tD, ALU.mult), [t_b, t_m])
            if bounce:
                t_b = emt("scalar", lambda e: e.copy(s_ln4[:, 3:4], tA[:, 0:1]), [t_b])
            ret = emt("vector", lambda e: e.tensor_tensor(out_ap, tA, tC, ALU.add), [t_b])
            g_qt[0] = ret
            return ret

        bank_guard = dict(bank_guard_init)

        def transpose_1024(src_bf, dst3, waits):
            tt = None
            for dc in range(8):
                wl = list(waits) + [init_ld] if dc == 0 else []
                if dc == 0:
                    wl.append(bank_guard.get("PT"))
                tt = emt("tensor", lambda e, dc=dc: e.transpose(
                    PT[:, dc * 128:(dc + 1) * 128], src_bf[:, dc * 128:(dc + 1) * 128],
                    s_eye[:]), waits=wl)
            tev = emt("scalar", lambda e: e.copy(dst3, PT[:].rearrange("p (dc n) -> p dc n", dc=8)),
                      waits=[tt])
            bank_guard["PT"] = tev
            return tev

        v_x = d_x.rearrange("(n p) d -> n p d", p=128)
        v_xq = d_xq.rearrange("(n p) d -> n p d", p=128)
        v_kS = d_kS.rearrange("hp p t -> p hp t")
        v_vS = d_vS.rearrange("(n p) f -> p n f", p=128)
        v_vSr = d_vS.rearrange("(n p) f -> n p f", p=128)
        v_qS = d_qS.rearrange("hp p t -> p hp t")
        v_x2S = d_x2S.rearrange("(n p) d -> n p d", p=128)
        v_x2Sp = d_x2S.rearrange("(n p) d -> p n d", p=128)
        v_w2T = d_w2T.rearrange("(n p) f -> n p f", p=128)
        v_w2qS = d_w2qS.rearrange("(n p) f -> p n f", p=128)
        v_w1T = d_w1T.rearrange("(c p) f -> p c f", p=128)
        v_w1bS = d_w1bS.rearrange("(c p) f -> p c f", p=128)
        v_y = d_y.rearrange("(n p) d -> n p d", p=128)

        PHASES = {"A1": 0, "A2": 1, "B": 2, "C": 3}
        _upto = PHASES[upto]
        g_vfm = [None, None]
        g_xst = [None, None]
        g_lnbf = [None, None]
        g_xln = [None, None]
        g_kst = [None, None]
        g_vst = [None, None]
        g_qst = [None, None]

        def ln_transpose_block(v_src, blk_idx, pb, sd_base, dbg_ln=False):
            xst = a32(FA0 + pb * 4096, 4096)
            xev = None
            tlds = []
            for c in range(4):
                tlds.append(dma("sync", SD[sd_base + pb], xst[:, c * 1024:(c + 1) * 1024],
                                v_src[blk_idx * 4 + c], waits=[g_xst[pb]] if c == 0 else []))
            for c in range(4):
                lnb = a16(A_LNBF + (c % 2) * 1024, 1024)
                tn = emit_ln(xst[:, c * 1024:(c + 1) * 1024], lnb,
                             waits=[tlds[-1], g_lnbf[c % 2], init_done, wload])
                dst3 = a16(A_XLN + pb * 4096, 4096).rearrange("p (dc n) -> p dc n", dc=8)
                xev = transpose_1024(lnb, dst3[:, :, c * 128:(c + 1) * 128],
                                     waits=[tn, g_xln[pb]] if c == 0 else [tn])
                g_lnbf[c % 2] = xev
                if dumps and "a" in _DS and dbg_ln:
                    g_lnbf[c % 2] = dma("sync", cDS, d_lnS[blk_idx * 4 + c], lnb, waits=[tn])
            if dumps and "a" in _DS and dbg_ln:
                dma("sync", cDS, d_xlnS[blk_idx], a16(A_XLN + pb * 4096, 4096), waits=[xev])
            g_xst[pb] = xev
            return xev

        def proj_fm(wofs, xln_off, ntok, nfc, pbanks, dst_off, waits_first):
            evs = []
            for fc in range(nfc):
                bk = pbanks[fc % 2]
                tm = None
                for dc in range(8):
                    wl = []
                    if dc == 0:
                        wl = [bank_guard.get(f"P{bk}")] + list(waits_first)
                    tm = emt("tensor", lambda e, fc=fc, dc=dc, bk=bk: e.matmul(
                        P[bk][:, 0:ntok],
                        a16(wofs + dc * 1024 + fc * 128, 128),
                        a16(xln_off + dc * ntok, ntok),
                        start=(dc == 0), stop=(dc == 7)), waits=wl)
                tev = emt("scalar", lambda e, fc=fc, bk=bk: e.copy(
                    a16(dst_off + fc * ntok, ntok), P[bk][:, 0:ntok]), waits=[tm])
                bank_guard[f"P{bk}"] = tev
                evs.append(tev)
            return evs

        for blkI in range(8):
            pb = blkI % 2
            import os as _os
            t_xln = ln_transpose_block(v_x, blkI, pb, sd_base=0,
                                       dbg_ln=(blkI < 8 and _os.environ.get("DBGLN", "1") == "1"))
            xln_off = A_XLN + pb * 4096
            if a1mode == "ln":
                g_xln[pb] = t_xln
                continue
            evs = proj_fm(WK_O, xln_off, 512, 8, (0, 1), A_KST + pb * 4096,
                          [t_xln, wload, g_kst[pb]])
            tkst = dma("sync", SD[2 + pb], v_kS[:, :, blkI * 512:(blkI + 1) * 512],
                       a16(A_KST + pb * 4096, 4096).rearrange("p (hp t) -> p hp t", hp=8),
                       waits=evs)
            g_kst[pb] = tkst
            evsv = proj_fm(WV_O, xln_off, 512, 8, (2, 3), A_QST + pb * 4096,
                           [t_xln, wload, g_vfm[pb]])
            vfm3 = a16(A_QST + pb * 4096, 4096).rearrange("p (fc t) -> p fc t", fc=8)
            tt = None
            for tc in range(4):
                for fc in range(8):
                    wl = [evsv[-1]]
                    if fc == 0:
                        wl.append(bank_guard.get("PT"))
                        wl.append(g_vst[pb])
                    tt = emt("tensor", lambda e, tc=tc, fc=fc, vfm3=vfm3: e.transpose(
                        PT[:, fc * 128:(fc + 1) * 128], vfm3[:, fc, tc * 128:(tc + 1) * 128],
                        s_eye[:]), waits=wl)
                tev = emt("scalar", lambda e, pb=pb: e.copy(
                    a16(A_VST + pb * 1024, 1024).rearrange("p (fc n) -> p fc n", fc=8),
                    PT[:].rearrange("p (fc n) -> p fc n", fc=8)), waits=[tt])
                bank_guard["PT"] = tev
                tvst = dma("sync", SD[4 + pb], v_vSr[blkI * 4 + tc],
                           a16(A_VST + pb * 1024, 1024), waits=[tev])
                g_vst[pb] = tvst
            g_xln[pb] = evsv[-1]
            g_vfm[pb] = tt
        for qb in range(4 if _upto >= 1 else 0):
            pb = qb % 2
            t_xln = ln_transpose_block(v_xq, qb, pb, sd_base=0)
            xln_off = A_XLN + pb * 4096
            evs = proj_fm(WQ_O, xln_off, 512, 8, (0, 1), A_QST + pb * 4096,
                          [t_xln, wload, g_qst[pb]])
            tq = dma("sync", SD[6 + pb], v_qS[:, :, qb * 512:(qb + 1) * 512],
                     a16(A_QST + pb * 4096, 4096).rearrange("p (hp t) -> p hp t", hp=8),
                     waits=evs)
            g_qst[pb] = tq
            g_xln[pb] = evs[-1]

        barrier()

        w2_chunk_idx = [0]
        g_w2st = [None, None]

        def emit_w2_chunk():
            wc = w2_chunk_idx[0]
            if wc >= 32:
                return
            w2_chunk_idx[0] += 1
            pb = wc % 2
            st = a32(FA0 + 6144 + pb * 1024, 1024)
            stb = a16(WQ_O + pb * 1024, 1024)
            tld = dma("sync", SD[2 + pb], st, v_w2T[wc], waits=[g_w2st[pb]])
            tq = emit_quant(st, stb, 3, [tld], a32(qt[0], 1024), a32(qt[1], 1024),
                            a32(qt[2], 1024), a32(qt[3], 1024))
            tst = dma("sync", SD[6], v_w2qS[:, wc, :], stb, waits=[tq])
            g_w2st[pb] = tst

        g_kc = [None, None]
        g_e = {}
        g_vall = None
        g_qblk = None
        g_afm = None
        g_xqst = [None, None]
        g_x2st = [None, None]
        g_epi = []
        g_njunk = [None]

        for Q in range(4 if _upto >= 2 else 0):
            L = 8 * Q + 8
            t_vall = []
            for gviii in range(0, L, 8):
                nL = min(8, L - gviii)
                t_vall.append(dma("sync", SD[9],
                                  a16(B_VAL + gviii * 1024, nL * 1024).rearrange(
                                      "p (n f) -> p n f", n=nL),
                                  v_vS[:, gviii:gviii + nL, :], waits=[g_vall]))
            t_qblk = dma("sync", SD[7], a16(B_QB, 4096).rearrange("p (hp t) -> p hp t", hp=8),
                         v_qS[:, :, Q * 512:(Q + 1) * 512], waits=[g_qblk])
            for hp in range(8):
                pbk = hp % 2
                t_kc = dma("sync", SD[pbk], a16(B_KC + pbk * 4096, L * 128),
                           v_kS[:, hp, 0:L * 128], waits=[g_kc[pbk]])
                t_scores_last = None
                t_attnv = [None, None]
                t_S = [None, None]
                for j in range(L):
                    nq = max(0, (j - 8 * Q) // 2)
                    off = nq * 128
                    sa, sb_ = (0, 1) if j % 2 == 0 else (2, 3)
                    ktile = a16(B_KC + pbk * 4096 + j * 128, 128)
                    wl_a = [t_kc, t_qblk, bank_guard.get(f"P{sa}")]
                    wl_b = [bank_guard.get(f"P{sb_}")]
                    tsc_a = emt("tensor", lambda e, sa=sa, off=off, hp=hp, kt=ktile: e.matmul(
                        P[sa][:, off:512], kt[0:64, :],
                        a16(B_QB + hp * 512 + off, 512 - off)[0:64, :],
                        start=True, stop=True), waits=wl_a)
                    tsc_b = emt("tensor", lambda e, sb_=sb_, off=off, hp=hp, kt=ktile: e.matmul(
                        P[sb_][:, off:512], kt[64:128, :],
                        a16(B_QB + hp * 512 + off, 512 - off)[64:128, :],
                        start=True, stop=True, tile_position=(64, 0)), waits=wl_b)
                    tm_a, tm_b = tsc_a, tsc_b
                    if j >= 8 * Q:
                        msk = s_md2 if (j - 8 * Q) % 2 == 0 else s_md1
                        tm_a = emt("vector", lambda e, sa=sa, off=off, m=msk: e.tensor_tensor(
                            P[sa][:, off:off + 128], P[sa][:, off:off + 128],
                            m[:], ALU.add), waits=[tsc_a, init_ld])
                        tm_b = emt("vector", lambda e, sb_=sb_, off=off, m=msk: e.tensor_tensor(
                            P[sb_][:, off:off + 128], P[sb_][:, off:off + 128],
                            m[:], ALU.add), waits=[tsc_b, init_ld])
                    pe = j % 2
                    e_a = a16(B_E + pe * 512, 512)
                    e_b = a16(B_E + 1024 + pe * 512, 512)
                    tex_a = emt("scalar", lambda e, sa=sa, off=off, ea=e_a: e.activation(
                        ea[:, off:512], P[sa][:, off:512], AF.Exp, scale=1.0 / 64.0),
                        waits=[tm_a, g_e.get(("a", pe))])
                    tex_b = emt("scalar", lambda e, sb_=sb_, off=off, eb=e_b: e.activation(
                        eb[:, off:512], P[sb_][:, off:512], AF.Exp, scale=1.0 / 64.0),
                        waits=[tm_b, g_e.get(("b", pe))])
                    bank_guard[f"P{sa}"] = tex_a
                    bank_guard[f"P{sb_}"] = tex_b
                    vt = a16(B_VAL + j * 1024 + hp * 128, 128)
                    wl_o = [tex_a] + t_vall
                    if j == 0:
                        wl_o.append(bank_guard.get(f"P{4 + pbk}"))
                    ta = emt("tensor", lambda e, pbk=pbk, off=off, vt=vt, ea=e_a, j=j, L=L: e.matmul(
                        P[4 + pbk][0:64, off:512], vt[:, 0:64], ea[:, off:512],
                        start=(j == 0), stop=(j == L - 1), tile_position=(0, 0),
                        skip_group_check=True), waits=wl_o)
                    wl_s = [tex_a]
                    if j == 0:
                        wl_s.append(bank_guard.get("P6"))
                    tSa = emt("tensor", lambda e, off=off, ea=e_a, j=j, L=L: e.matmul(
                        P[6][0:1, off:512], s_ones[:, 0:1], ea[:, off:512],
                        start=(j == 0), stop=(j == L - 1), skip_group_check=True), waits=wl_s)
                    tb = emt("tensor", lambda e, pbk=pbk, off=off, vt=vt, eb=e_b, j=j, L=L: e.matmul(
                        P[4 + pbk][64:128, off:512], vt[:, 64:128], eb[:, off:512],
                        start=(j == 0), stop=(j == L - 1), tile_position=(0, 64),
                        skip_group_check=True), waits=[tex_b])
                    tSb = emt("tensor", lambda e, off=off, eb=e_b, j=j, L=L: e.matmul(
                        P[6][32:33, off:512], s_ones[:, 1:2], eb[:, off:512],
                        start=(j == 0), stop=(j == L - 1), tile_position=(0, 32),
                        skip_group_check=True), waits=[tex_b])
                    t_attnv = [ta, tb]
                    t_S = [tSa, tSb]
                    g_e[("a", pe)] = tSa
                    g_e[("b", pe)] = tSb
                    if dumps and "b" in _DS and Q == 0 and j < 2:
                        g_e[("a", pe)] = dma("sync", cDS, d_eS[hp][j * 2], e_a,
                                             waits=[tSa, tb])
                        g_e[("b", pe)] = dma("sync", cDS, d_eS[hp][j * 2 + 1], e_b,
                                             waits=[tSb])
                    t_scores_last = (tsc_a, tsc_b)
                g_kc[pbk] = t_scores_last[1]
                tr_a = emt("vector", lambda e: e.reciprocal(s_Bb[0:1, :], P[6][0:1, :]),
                           waits=t_S + t_attnv)
                tr_b = emt("vector", lambda e: e.reciprocal(s_Bb[32:33, :], P[6][32:33, :]),
                           waits=[])
                bank_guard["P6"] = tr_b
                tr_a = emt("scalar", lambda e: e.copy(s_rS[0:1, :], s_Bb[0:1, :]), waits=[tr_a])
                tr_b = emt("scalar", lambda e: e.copy(s_rS[32:33, :], s_Bb[32:33, :]), waits=[tr_b])
                wl_B = [tr_a, tr_b, bank_guard.get("P0")]
                tB1 = emt("tensor", lambda e: e.matmul(P[0][0:64, :], s_ones[0:1, 0:64],
                                                       s_rS[0:1, :], start=True, stop=True,
                                                       skip_group_check=True), waits=wl_B)
                tB2 = emt("tensor", lambda e: e.matmul(P[0][64:128, :], s_ones[32:33, 0:64],
                                                       s_rS[32:33, :], start=True, stop=True,
                                                       tile_position=(32, 64),
                                                       skip_group_check=True), waits=[])
                tBe = emt("scalar", lambda e: e.copy(s_Bb[:], P[0][:]), waits=[tB1, tB2])
                bank_guard["P0"] = tBe
                if dumps and "b" in _DS and Q == 0:
                    tBe = dma("sync", cDS, d_sbbS[hp], s_Bb[:], waits=[tBe])
                    bank_guard["P0"] = tBe
                tno = emt("vector", lambda e, hp=hp, pbk=pbk: e.tensor_tensor(
                    a16(B_AFM + hp * 512, 512), P[4 + pbk][:], s_Bb[:], ALU.mult),
                    waits=[tBe, g_afm] + t_attnv)
                bank_guard[f"P{4 + pbk}"] = tno
                if dumps and "b" in _DS and Q == 0:
                    t_nc = emt("scalar", lambda e, pbk=pbk: e.copy(
                        s_junk[:, 0:512], P[4 + pbk][:]), waits=[tno, g_njunk[0]])
                    g_njunk[0] = dma("sync", cDS, d_numS[hp], s_junk[:, 0:512],
                                     waits=[t_nc])
                    bank_guard[f"P{4 + pbk}"] = g_njunk[0]
                if hp == 0:
                    afm_toks = []
                afm_toks.append(tno)
                emit_w2_chunk()
            t_afm_dump = None
            if dumps and "b" in _DS:
                t_afm_dump = dma("sync", cDS, d_afmS[Q], a16(B_AFM, 4096), waits=afm_toks)
            wo_last = None
            for tc in range(4):
                ptc = tc % 2
                xqst = a32(FA0 + ptc * 1024, 1024)
                t_xq = dma("sync", SD[4 + ptc], xqst, v_xq[Q * 4 + tc], waits=[g_xqst[ptc]])
                tW = [None, None]
                for nh in range(2):
                    bk = 4 + nh
                    tm = None
                    for dc in range(8):
                        wl = afm_toks + [bank_guard.get(f"P{bk}")] if dc == 0 else []
                        tm = emt("tensor", lambda e, tc=tc, nh=nh, dc=dc, bk=bk: e.matmul(
                            P[bk][:], a16(B_AFM + dc * 512 + tc * 128, 128),
                            a16(WO_O + dc * 1024 + nh * 512, 512),
                            start=(dc == 0), stop=(dc == 7)), waits=wl)
                    tW[nh] = tm
                    wo_last = tm
                o1 = a32(FA0 + 2048, 1024)
                to1a = emt("vector", lambda e, o1=o1: e.tensor_tensor(
                    o1[:, 0:512], P[4][:], s_bo[:, 0:512], ALU.add), waits=[tW[0], t_bo])
                to1b = emt("vector", lambda e, o1=o1: e.tensor_tensor(
                    o1[:, 512:1024], P[5][:], s_bo[:, 512:1024], ALU.add), waits=[tW[1]])
                bank_guard["P4"] = to1a
                bank_guard["P5"] = to1b
                if dumps and "b" in _DS:
                    t_do1 = dma("sync", cDS, d_epiS[Q * 4 + tc][1], o1,
                                waits=[to1a, to1b])
                    bank_guard["P4"] = t_do1
                o2 = a32(FA0 + 3072, 1024)
                tln = emit_ln(o1, o2, waits=[to1a, to1b] + g_epi)
                x2 = a32(FA0 + 4096 + ptc * 1024, 1024)
                tx2 = emt("vector", lambda e, x2=x2, xqst=xqst, o2=o2: e.tensor_tensor(
                    x2, xqst, o2, ALU.add), waits=[tln, t_xq, g_x2st[ptc]])
                g_xqst[ptc] = tx2
                if dumps and "b" in _DS:
                    t_dxq = dma("sync", cDS, d_epiS[Q * 4 + tc][0], xqst, waits=[tx2])
                    t_do2 = dma("sync", cDS, d_epiS[Q * 4 + tc][2], o2, waits=[tx2])
                    g_xqst[ptc] = t_dxq
                    g_epi = [t_do2]
                tst = dma("sync", cDS, v_x2S[Q * 4 + tc], x2, waits=[tx2])
                g_x2st[ptc] = tst
            g_vall = t_attnv[0]
            g_qblk = t_scores_last[0]
            g_afm = wo_last if t_afm_dump is None else t_afm_dump
        while _upto >= 2 and w2_chunk_idx[0] < 32:
            emit_w2_chunk()

        barrier()
        bank_guard.clear()

        t_w2q = []
        for i in range(4 if _upto >= 3 else 0):
            t_w2q.append(dma("sync", SD[i], a16(C_W2Q + i * 8192, 8192).rearrange(
                "p (n f) -> p n f", n=8), v_w2qS[:, i * 8:(i + 1) * 8, :]))
        g_w1st = [None, None]
        t_w1c = []
        for c8 in range(8 if _upto >= 3 else 0):
            pb = c8 % 2
            stg = a16(C_W1G + pb * 4096, 4096)
            tcs = []
            for hf in range(2):
                stf = a32(FA0 + (2 * c8 + hf) % 2 * 2048, 2048)
                tin = dma("sync", SD[8 + (2 * c8 + hf) % 2], stf,
                          v_w1T[:, c8, hf * 2048:(hf + 1) * 2048], waits=[g_w1st[(2 * c8 + hf) % 2]])
                tc_ = emt("scalar", lambda e, stg=stg, stf=stf, hf=hf: e.copy(
                    stg[:, hf * 2048:(hf + 1) * 2048], stf), waits=[tin])
                g_w1st[(2 * c8 + hf) % 2] = tc_
                tcs.append(tc_)
            tout = dma("sync", cDS, v_w1bS[:, c8, :], stg, waits=tcs)
            t_w1c.append(tout)
        tb2q = None
        if _upto >= 3:
            tb2q = emit_quant(s_b2[:], s_b2q[:], 10, [init_ld], a32(qt[0], 8), a32(qt[1], 8),
                              a32(qt[2], 8), a32(qt[3], 8), bounce=True)

        g_x2c = [t_w1c[-1] if t_w1c else None, t_w1c[-1] if t_w1c else None]
        g_w1g = [None, None]
        g_hq = [None] * 8
        g_hqd = [None] * 8
        g_h = [None, None]
        g_yst = [None, None]
        g_xln2 = []
        g_lnb = [[], []]
        g_x2cd = [None, None]
        for T in range(8 if _upto >= 3 else 0):
            pT = T % 2
            x2c = a32(FA0 + pT * 2048, 2048)
            t_x2 = dma("sync", SD[4 + pT], x2c.rearrange("p (n f) -> p n f", n=2),
                       v_x2Sp[:, T * 2:T * 2 + 2, :], waits=[g_x2c[pT], g_x2cd[pT]])
            xev = None
            xevs = []
            if dumps and "c1" in _DS:
                g_x2cd[pT] = dma("sync", cDS, d_x2cS[T], x2c, waits=[t_x2])
            for tc2 in range(2):
                lnb = a16(C_XLN2 + 2048 + (tc2 % 2) * 1024, 1024)
                tn = emit_ln(x2c[:, tc2 * 1024:(tc2 + 1) * 1024], lnb,
                             waits=[t_x2] + g_lnb[tc2])
                if dumps and "c1" in _DS:
                    tn = dma("sync", cDS, d_lnbS[T][tc2], lnb, waits=[tn])
                dst3 = a16(C_XLN2, 2048).rearrange("p (dc n) -> p dc n", dc=8)
                xev = transpose_1024(lnb, dst3[:, :, tc2 * 128:(tc2 + 1) * 128],
                                     waits=[tn] + g_xln2)
                g_lnb[tc2] = [xev]
                xevs.append(xev)
            if dumps and "c1" in _DS:
                g_xln2 = [dma("sync", cDS, d_xln2S[T], a16(C_XLN2, 2048), waits=xevs)]
            ffn2_last = None
            t_hqs = []
            lastA = [None] * 4
            for hg in range(8):
                phg = hg % 2
                w1g = a16(C_W1G + phg * 4096, 4096)
                t_w1 = dma("sync", SD[6 + phg], w1g.rearrange("p (c f) -> p c f", c=8),
                           v_w1bS[:, :, hg * 512:(hg + 1) * 512],
                           waits=[g_w1g[phg]] + (t_w1c if T == 0 and hg < 2 else []))
                hsb = a32(FA0 + 4096 + phg * 1024, 1024)
                t_h = []
                for hc in range(4):
                    bk = 4 + hc % 2
                    tm = None
                    for dc in range(8):
                        wl = []
                        if dc == 0:
                            wl = [bank_guard.get(f"P{bk}"), t_w1, xev]
                        tm = emt("tensor", lambda e, hc=hc, dc=dc, bk=bk, phg=phg: e.matmul(
                            P[bk][:, 0:256], a16(C_W1G + phg * 4096 + dc * 512 + hc * 128, 128),
                            a16(C_XLN2 + dc * 256, 256), start=(dc == 0), stop=(dc == 7)),
                            waits=wl)
                    trl = emt("scalar", lambda e, hc=hc, bk=bk, hg=hg, hsb=hsb: e.activation(
                        hsb[:, hc * 256:(hc + 1) * 256], P[bk][:, 0:256], AF.Relu,
                        bias=s_b1[:, hg * 4 + hc:hg * 4 + hc + 1]),
                        waits=[tm, init_ld, g_h[phg] if hc == 0 else None])
                    bank_guard[f"P{bk}"] = trl
                    t_h.append(trl)
                hq = a16(C_HQ + hg * 1024, 1024)
                t_hq = emit_quant(hsb, hq, 3, t_h + [g_hq[hg], g_hqd[hg]],
                                  a32(qt[0], 1024), a32(qt[1], 1024), a32(qt[2], 1024),
                                  a32(qt[3], 1024))
                g_h[phg] = t_hq
                t_hqs.append(t_hq)
                if dumps and "c2" in _DS:
                    g_hqd[hg] = dma("sync", cDS, d_hqS[T][hg], hq, waits=[t_hq])
                for hc in range(4):
                    for fc in range(4):
                        wl = [t_hq] + t_w2q
                        if hg == 0 and hc == 0:
                            wl.append(bank_guard.get(f"PF{fc}"))
                        tfa = emt("tensor", lambda e, hc=hc, fc=fc, hg=hg: e.matmul(
                            P[fc][:, 0:256],
                            a16(C_W2Q + (hg * 4 + hc) * 1024 + fc * 128, 128),
                            a16(C_HQ + hg * 1024 + hc * 256, 256),
                            start=(hg == 0 and hc == 0),
                            stop=(hg == 7 and hc == 3), skip_group_check=True), waits=wl)
                        lastA[fc] = tfa
                ffn1_last = tm
                g_w1g[phg] = tm
            o_fm = a32(FA0 + 6144, 2048)
            t_ev = []
            for fc in range(4):
                tev_ = emt("vector", lambda e, fc=fc, o_fm=o_fm: e.tensor_scalar(
                    o_fm[:, fc * 256:(fc + 1) * 256], P[fc][:, 0:256],
                    s_b2q[:, fc:fc + 1], None, ALU.add),
                    waits=[lastA[fc], tb2q, g_yst[0], g_yst[1]])
                bank_guard[f"PF{fc}"] = tev_
                t_ev.append(tev_)
            lastB = [None] * 4
            for hg in range(8):
                for hc in range(4):
                    for fc in range(4, 8):
                        wl = []
                        if hg == 0 and hc == 0:
                            wl = [bank_guard.get(f"PF{fc - 4}"), t_hqs[-1]]
                        tfb = emt("tensor", lambda e, hc=hc, fc=fc, hg=hg: e.matmul(
                            P[fc - 4][:, 0:256],
                            a16(C_W2Q + (hg * 4 + hc) * 1024 + fc * 128, 128),
                            a16(C_HQ + hg * 1024 + hc * 256, 256),
                            start=(hg == 0 and hc == 0),
                            stop=(hg == 7 and hc == 3), skip_group_check=True), waits=wl)
                        lastB[fc - 4] = tfb
                g_hq[hg] = lastB[3]
            ffn2_last = lastB[3]
            for fc in range(4, 8):
                tev_ = emt("vector", lambda e, fc=fc, o_fm=o_fm: e.tensor_scalar(
                    o_fm[:, fc * 256:(fc + 1) * 256], P[fc - 4][:, 0:256],
                    s_b2q[:, fc:fc + 1], None, ALU.add),
                    waits=[lastB[fc - 4], g_yst[0], g_yst[1]])
                bank_guard[f"PF{fc - 4}"] = tev_
                t_ev.append(tev_)
            if dumps and "c3" in _DS:
                t_ev = t_ev + [dma("sync", cDS, d_ofmS[T][0], o_fm, waits=t_ev)]
            tq1 = emit_quant(o_fm[:, 0:1024], o_fm[:, 0:1024], 10, t_ev, a32(qt[0], 1024),
                             a32(qt[1], 1024), a32(qt[2], 1024), a32(qt[3], 1024))
            tq2 = emit_quant(o_fm[:, 1024:2048], o_fm[:, 1024:2048], 10, t_ev, a32(qt[0], 1024),
                             a32(qt[1], 1024), a32(qt[2], 1024), a32(qt[3], 1024))
            if dumps and "c3" in _DS:
                tq2 = dma("sync", cDS, d_ofmS[T][1], o_fm, waits=[tq1, tq2])
            o_tm = a32(FA0 + 4096, 2048)
            for tc2 in range(2):
                for fcg in range(2):
                    tt = None
                    for i in range(4):
                        fc = fcg * 4 + i
                        wl = [tq1, tq2, bank_guard.get("P6"), init_ld] if i == 0 else []
                        tt = emt("tensor", lambda e, fc=fc, tc2=tc2, i=i, o_fm=o_fm: e.transpose(
                            P[6][:, i * 128:(i + 1) * 128],
                            o_fm[:, fc * 256 + tc2 * 128: fc * 256 + tc2 * 128 + 128],
                            s_eyef[:]), waits=wl)
                    tev = emt("vector", lambda e, tc2=tc2, fcg=fcg, o_tm=o_tm: e.tensor_copy(
                        o_tm[:, tc2 * 1024 + fcg * 512: tc2 * 1024 + fcg * 512 + 512],
                        P[6][:]), waits=[tt])
                    bank_guard["P6"] = tev
            g_otmd = None
            if dumps and "c3" in _DS:
                g_otmd = dma("sync", cDS, d_otmS[T], o_tm, waits=[bank_guard["P6"]])
            o3 = a32(FA0 + 6144, 1024)
            yst = a32(FA0 + 7168, 1024)
            tlns = []
            t_prev_y = None
            for tc2 in range(2):
                tln = emit_ln(o_tm[:, tc2 * 1024:(tc2 + 1) * 1024], o3,
                              waits=[bank_guard["P6"], g_yst[tc2], g_otmd])
                tlns.append(tln)
                tx3 = emt("vector", lambda e, tc2=tc2, x2c=x2c, o3=o3, yst=yst: e.tensor_tensor(
                    yst, x2c[:, tc2 * 1024:(tc2 + 1) * 1024], o3, ALU.add),
                    waits=[tln, t_prev_y])
                tyD = dma("sync", SD[8 + tc2], v_y[T * 2 + tc2], yst, waits=[tx3])
                t_prev_y = tyD
                g_yst[tc2] = tyD
            g_x2c[pT] = g_yst[1]
            g_w1g = [ffn2_last, ffn2_last]
            g_h = [tlns[0], tlns[1]]
            g_xln2 = (g_xln2 if dumps else []) + [ffn1_last]

        barrier()

        import os as _os
        DEDUP = _os.environ.get("NODEDUP", "0") != "1"
        last_wait = {}

        def body(engname):
            def run(eng):
                for (en, fn, waits, inc) in order:
                    if en != engname:
                        continue
                    for (ctr, v) in waits:
                        if ctr is ENG_CTR.get(engname):
                            continue
                        key = (engname, id(ctr))
                        if DEDUP and last_wait.get(key, -1) >= v:
                            continue
                        eng.wait_ge(ctr.sem, v * ctr.step)
                        last_wait[key] = v
                    if fn is not None:
                        instr = fn(eng)
                        if inc is not None:
                            instr.then_inc(inc.sem, inc.step)
            return run

        blk = es.enter_context(nc.Block())
        blk.sync(body("sync"))
        blk.tensor(body("tensor"))
        blk.vector(body("vector"))
        blk.scalar(body("scalar"))
        blk.gpsimd(body("gpsimd"))
    return nc


_NC = None


def kernel(L0_attn_x, L0_attn_Wq, L0_attn_Wk, L0_attn_Wv, L0_attn_Wo, L0_attn_bo,
           L0_ffn_w1, L0_ffn_b1, L0_ffn_w2, L0_ffn_b2):
    global _NC
    x = np.ascontiguousarray(np.asarray(L0_attn_x, dtype=np.float32))
    wqT = np.ascontiguousarray(np.asarray(L0_attn_Wq, np.float32).T)
    wkT = np.ascontiguousarray(np.asarray(L0_attn_Wk, np.float32).T)
    wvT = np.ascontiguousarray(np.asarray(L0_attn_Wv, np.float32).T)
    woT = np.ascontiguousarray(np.asarray(L0_attn_Wo, np.float32).T)
    w1T = np.ascontiguousarray(np.asarray(L0_ffn_w1, np.float32).T)
    w2T = np.ascontiguousarray(np.asarray(L0_ffn_w2, np.float32).T)
    b1 = np.ascontiguousarray(np.asarray(L0_ffn_b1, np.float32).reshape(32, 128).T)
    b2 = np.ascontiguousarray(np.asarray(L0_ffn_b2, np.float32).reshape(8, 128).T)
    bo = np.ascontiguousarray(np.asarray(L0_attn_bo, np.float32))

    eye = np.eye(128, dtype=ml_dtypes.bfloat16)
    eyef = np.eye(128, dtype=np.float32)
    kk, qq = np.meshgrid(np.arange(128), np.arange(128), indexing="ij")
    tri = np.where(kk > qq, np.float32(-1.0e30), np.float32(0.0))
    zer = np.zeros((128, 128), np.float32)
    ful = np.full((128, 128), np.float32(-1.0e30))

    if _NC is None:
        _NC = build_nc()

    in_maps = []
    slot_rows = {}
    for c in range(8):
        b, h = c // 2, c % 2
        own = [2 * s + (1 - h) for s in range(16)]
        rows = np.concatenate([np.arange(t * 128, (t + 1) * 128) for t in own])
        slot_rows[c] = rows
        xq = np.ascontiguousarray(x[b][rows])
        md1, md2 = (tri, zer) if h == 0 else (ful, tri)
        in_maps.append({
            "x": np.ascontiguousarray(x[b]), "xq": xq,
            "wqT": wqT, "wkT": wkT, "wvT": wvT, "woT": woT,
            "w1T": w1T, "w2T": w2T, "b1": b1, "b2": b2, "bo": bo,
            "eye": eye, "eyef": eyef, "md1": md1, "md2": md2,
        })

    res = run_bass_kernel_spmd(_NC, in_maps, core_ids=list(range(8)))
    out = np.empty((B, S, D), dtype=np.float32)
    for c in range(8):
        b = c // 2
        out[b][slot_rows[c]] = res.results[c]["y"]
    return out


# version 04
# speedup vs baseline: 1.0006x; 1.0006x over previous
import contextlib
import numpy as np
import ml_dtypes
import concourse.bass as bass
import concourse.mybir as mybir
from concourse.bass_utils import run_bass_kernel_spmd

F32 = mybir.dt.float32
BF16 = mybir.dt.bfloat16
AF = mybir.ActivationFunctionType
ALU = mybir.AluOpType

B, S, D, H, HD, FF = 4, 4096, 1024, 16, 64, 4096


class Ctr:
    __slots__ = ("sem", "step", "n")

    def __init__(self, sem, step):
        self.sem, self.step, self.n = sem, step, 0


def build_nc(debug=False, upto="C", a1mode="full", serial=False, dumps=None):
    if dumps is None:
        dumps = debug
    import os as _os0
    _DS = set(_os0.environ.get("DUMPSET", "a,b,c1,c2,c3").split(","))
    nc = bass.Bass("TRN2", target_bir_lowering=False)
    SCRATCH_KIND = "ExternalOutput" if debug else "Internal"

    d_x = nc.dram_tensor("x", [S, D], F32, kind="ExternalInput").ap()
    d_xq = nc.dram_tensor("xq", [2048, D], F32, kind="ExternalInput").ap()
    d_wqT = nc.dram_tensor("wqT", [D, D], F32, kind="ExternalInput").ap()
    d_wkT = nc.dram_tensor("wkT", [D, D], F32, kind="ExternalInput").ap()
    d_wvT = nc.dram_tensor("wvT", [D, D], F32, kind="ExternalInput").ap()
    d_woT = nc.dram_tensor("woT", [D, D], F32, kind="ExternalInput").ap()
    d_w1T = nc.dram_tensor("w1T", [D, FF], F32, kind="ExternalInput").ap()
    d_w2T = nc.dram_tensor("w2T", [FF, D], F32, kind="ExternalInput").ap()
    d_b1 = nc.dram_tensor("b1", [128, 32], F32, kind="ExternalInput").ap()
    d_b2 = nc.dram_tensor("b2", [128, 8], F32, kind="ExternalInput").ap()
    d_bo = nc.dram_tensor("bo", [D], F32, kind="ExternalInput").ap()
    d_eye = nc.dram_tensor("eye", [128, 128], BF16, kind="ExternalInput").ap()
    d_eyef = nc.dram_tensor("eyef", [128, 128], F32, kind="ExternalInput").ap()
    d_md1 = nc.dram_tensor("md1", [128, 128], F32, kind="ExternalInput").ap()
    d_md2 = nc.dram_tensor("md2", [128, 128], F32, kind="ExternalInput").ap()
    d_y = nc.dram_tensor("y", [2048, D], F32, kind="ExternalOutput").ap()
    d_kS = nc.dram_tensor("kS", [8, 128, S], BF16, kind=SCRATCH_KIND).ap()
    d_vS = nc.dram_tensor("vS", [S, D], BF16, kind=SCRATCH_KIND).ap()
    d_qS = nc.dram_tensor("qS", [8, 128, 2048], BF16, kind=SCRATCH_KIND).ap()
    d_x2S = nc.dram_tensor("x2S", [2048, D], F32, kind=SCRATCH_KIND).ap()
    d_w2qS = nc.dram_tensor("w2qS", [FF, D], BF16, kind=SCRATCH_KIND).ap()
    d_w1bS = nc.dram_tensor("w1bS", [D, FF], BF16, kind=SCRATCH_KIND).ap()
    if debug:
        d_lnS = nc.dram_tensor("lnS", [32, 128, D], BF16, kind="ExternalOutput").ap()
        d_xlnS = nc.dram_tensor("xlnS", [8, 128, 8 * 512], BF16, kind="ExternalOutput").ap()
        d_afmS = nc.dram_tensor("afmS", [4, 128, 4096], BF16, kind="ExternalOutput").ap()
        d_epiS = nc.dram_tensor("epiS", [16, 3, 128, D], F32, kind="ExternalOutput").ap()
        d_sbbS = nc.dram_tensor("sbbS", [8, 128, 512], F32, kind="ExternalOutput").ap()
        d_numS = nc.dram_tensor("numS", [8, 128, 512], F32, kind="ExternalOutput").ap()
        d_eS = nc.dram_tensor("eS", [8, 4, 128, 512], BF16, kind="ExternalOutput").ap()
        d_xln2S = nc.dram_tensor("xln2S", [8, 128, 2048], BF16, kind="ExternalOutput").ap()
        d_x2cS = nc.dram_tensor("x2cS", [8, 128, 2048], F32, kind="ExternalOutput").ap()
        d_lnbS = nc.dram_tensor("lnbS", [8, 2, 128, 1024], BF16, kind="ExternalOutput").ap()
        d_hqS = nc.dram_tensor("hqS", [8, 8, 128, 1024], BF16, kind="ExternalOutput").ap()
        d_ofmS = nc.dram_tensor("ofmS", [8, 2, 128, 2048], F32, kind="ExternalOutput").ap()
        d_otmS = nc.dram_tensor("otmS", [8, 128, 2048], F32, kind="ExternalOutput").ap()

    es = contextlib.ExitStack()
    with es:
        sbuf = lambda n, sh, dt: es.enter_context(nc.sbuf_tensor(n, sh, dt))
        psumt = lambda n, sh, dt: es.enter_context(nc.psum_tensor(n, sh, dt))
        sem = lambda n: es.enter_context(nc.semaphore(n))

        AR16 = sbuf("AR16", [128, 68096], BF16)
        AR32 = sbuf("AR32", [128, 12544], F32)
        s_eye = sbuf("s_eye", [128, 128], BF16)
        s_eyef = sbuf("s_eyef", [128, 128], F32)
        s_md1 = sbuf("s_md1", [128, 128], F32)
        s_md2 = sbuf("s_md2", [128, 128], F32)
        s_b1 = sbuf("s_b1", [128, 32], F32)
        s_b2 = sbuf("s_b2", [128, 8], F32)
        s_b2q = sbuf("s_b2q", [128, 8], F32)
        s_bo = sbuf("s_bo", [128, 1024], F32)
        s_stat = sbuf("s_stat", [128, 12], F32)
        s_mv = sbuf("s_mv", [128, 2], F32)
        s_rstd = sbuf("s_rstd", [128, 1], F32)
        s_ln4 = sbuf("s_ln4", [128, 4], F32)
        s_junk = sbuf("s_junk", [128, 2048], F32)
        s_K = sbuf("s_K", [128, 8], F32)
        s_ones = sbuf("s_ones", [128, 128], BF16)
        s_bor = sbuf("s_bor", [1, 1024], BF16)
        s_rS = sbuf("s_rS", [128, 512], BF16)
        s_Bb = sbuf("s_Bb", [128, 512], F32)
        s_cst = sbuf("s_cst", [128, 2], F32)

        P = [psumt(f"P{i}", [128, 512], F32) for i in range(7)]
        PT = psumt("PT", [128, 1024], BF16)

        cT = Ctr(sem("smT"), 1)
        cV = Ctr(sem("smV"), 1)
        cS = Ctr(sem("smS"), 1)
        cG = Ctr(sem("smG"), 1)
        ENG_CTR = {"tensor": cT, "vector": cV, "scalar": cS, "gpsimd": cG}
        cDI = Ctr(sem("sdI"), 16)
        cGI = Ctr(sem("sgI"), 16)
        cDS = Ctr(sem("sdS"), 16)
        cGS = Ctr(sem("sgS"), 16)
        SD = [Ctr(sem(f"sd{i}"), 16) for i in range(10)]
        GDp = [Ctr(sem(f"sg{i}"), 16) for i in range(2)]
        ALL_CTRS = [cT, cV, cS, cG, cDI, cGI, cDS, cGS] + SD + GDp

        order = []
        _prev = [None]

        def em(eng, fn, waits=(), inc=None):
            if serial and fn is not None and inc is None:
                inc = ENG_CTR.get(eng) or (cD if eng == "sync" else cG)
            tok = None
            if inc is not None:
                inc.n += 1
                tok = (inc, inc.n)
            waits = tuple(w for w in waits if w is not None)
            if serial and fn is not None:
                waits = waits + ((_prev[0],) if _prev[0] else ())
                _prev[0] = tok
            order.append((eng, fn, waits, inc))
            return tok

        def emt(eng, fn, waits=()):
            return em(eng, fn, waits, inc=ENG_CTR[eng])

        def dma(eng, ctr, out_ap, in_ap, waits=(), slow=False):
            return em(eng, lambda e, o=out_ap, i=in_ap, s=slow: e.dma_start(
                o, i, allow_slow_non_contiguous=s), waits, inc=ctr)

        def barrier():
            snap = [(c, c.n) for c in ALL_CTRS if c.n > 0]
            for eng in ("sync", "tensor", "vector", "scalar", "gpsimd"):
                em(eng, None, waits=snap)

        WO_O, WQ_O, WK_O, WV_O, SCR_O, E_O = 0, 8192, 16384, 24576, 32768, 65536
        A_XLN, A_KST, A_VST, A_QST = SCR_O, SCR_O + 8192, SCR_O + 16384, SCR_O + 18432
        A_LNBF = E_O
        C_W2Q, C_W1G, C_HQ, C_XLN2 = SCR_O, WQ_O, WK_O, WV_O
        B_KC, B_QB, B_AFM, B_VAL, B_E = WK_O, WV_O, WV_O + 4096, SCR_O, E_O
        FA0 = 0
        qt = [8192 + i * 1024 for i in range(4)]

        def a16(off, n):
            assert off + n <= 68096, (off, n)
            return AR16[:, off:off + n]

        def a32(off, n):
            assert off + n <= 12544, (off, n)
            return AR32[:, off:off + n]

        nc.const_aps.register(F32, 1e-5, s_cst[:, 0:1])
        em("vector", lambda e: e.memset(s_cst[:, 0:1], 1e-5))
        em("vector", lambda e: e.memset(s_ones[:], 1.0))
        kcol = lambda j: s_K[:, j:j + 1]
        tt0 = emt("vector", lambda e: e.memset(kcol(0), float(2 ** 25)))
        tt0 = emt("scalar", lambda e: e.copy(kcol(6), kcol(0)), [tt0])
        tK = emt("vector", lambda e: e.tensor_scalar(kcol(1), kcol(6), 1.0, kcol(6),
                                                     ALU.add, ALU.subtract), [tt0])
        tK = emt("scalar", lambda e: e.copy(kcol(6), kcol(1)), [tK])

        def ceff(j, Ash, Bnat, tprev):
            v1 = emt("vector", lambda e, A=Ash: e.tensor_scalar(kcol(7), kcol(1), float(A),
                                                                None, ALU.mult), [tprev, tK])
            s1 = emt("scalar", lambda e: e.copy(kcol(6), kcol(7)), [v1])
            v2 = emt("vector", lambda e, Bn=Bnat, j=j: e.tensor_scalar(
                kcol(j), kcol(1), float(-Bn), float(Bn), ALU.mult, ALU.add), [s1])
            s2 = emt("scalar", lambda e, j=j: e.copy(kcol(6), kcol(j)), [v2])
            v3 = emt("vector", lambda e, j=j: e.tensor_tensor(kcol(j), kcol(j), kcol(7),
                                                              ALU.add), [s2])
            return emt("scalar", lambda e, j=j: e.copy(kcol(6), kcol(j)), [v3])

        t2_ = ceff(2, 2.0 ** 49, float(2 ** 20 + 1), tK)
        t3_ = ceff(3, 3.0 * 2 ** 51, float(3 * 2 ** 22), t2_)
        t4_ = ceff(4, 2.0 ** 42, float(2 ** 13 + 1), t3_)
        setup_tok = emt("scalar", lambda e: e.copy(kcol(5), kcol(4)), [t4_])

        dma("sync", cDI, s_eye[:], d_eye)
        dma("sync", cDI, s_eyef[:], d_eyef)
        dma("sync", cDI, s_md1[:], d_md1)
        dma("sync", cDI, s_md2[:], d_md2)
        dma("sync", cDI, s_b1[:], d_b1)
        dma("sync", cDI, s_b2[:], d_b2)
        init_ld = dma("sync", cDI, s_bo[0:1, :], d_bo.rearrange("(o f) -> o f", o=1))
        t_boc = emt("scalar", lambda e: e.copy(s_bor[:], s_bo[0:1, :]), waits=[init_ld])

        tbm1 = emt("tensor", lambda e: e.matmul(P[0][:], s_ones[0:1, :], s_bor[0:1, 0:512],
                                                start=True, stop=True), waits=[t_boc, setup_tok])
        tbm2 = emt("tensor", lambda e: e.matmul(P[1][:], s_ones[0:1, :], s_bor[0:1, 512:1024],
                                                start=True, stop=True), waits=[])
        tbe1 = emt("scalar", lambda e: e.copy(s_bo[:, 0:512], P[0][:]), waits=[tbm1])
        t_bo = emt("scalar", lambda e: e.copy(s_bo[:, 512:1024], P[1][:]), waits=[tbm2])
        bank_guard_init = {"P0": tbe1, "P1": t_bo}
        init_done = t_bo
        wload = None
        _wst_g = [None, None]
        _wlist = [] if a1mode == "ln" else [(WO_O, d_woT), (WK_O, d_wkT), (WV_O, d_wvT), (WQ_O, d_wqT)]
        for wi, (wofs, dw) in enumerate(_wlist):
            for qtr in range(4):
                pw = (wi * 4 + qtr) % 2
                stg = a32(8192 + pw * 2048, 2048)
                tld = dma("sync", SD[8 + pw], stg.rearrange("p (c f) -> p c f", c=2),
                          dw.rearrange("(c p) f -> p c f", p=128)[:, qtr * 2:qtr * 2 + 2, :],
                          waits=[_wst_g[pw]])
                wload = emt("scalar", lambda e, wofs=wofs, qtr=qtr, stg=stg: e.copy(
                    a16(wofs + qtr * 2048, 2048), stg), waits=[tld])
                _wst_g[pw] = wload

        g_lnj = [None]

        def emit_ln(src_ap, dst_ap, waits):
            t1 = emt("vector", lambda e: e.tensor_reduce(s_ln4[:, 0:1], src_ap,
                                                         mybir.AxisListType.X, ALU.add), waits)
            t2 = emt("scalar", lambda e: e.activation(s_junk[:, 1024:2048], src_ap, AF.Square),
                     list(waits) + [g_lnj[0]])
            t2 = emt("vector", lambda e: e.tensor_reduce(s_ln4[:, 1:2], s_junk[:, 1024:2048],
                                                         mybir.AxisListType.X, ALU.add), [t2])
            g_lnj[0] = t2
            tm = emt("scalar", lambda e: e.activation(s_mv[:, 0:1], s_ln4[:, 0:1], AF.Copy,
                                                      scale=1.0 / 1024.0), [t1])
            tmm = emt("vector", lambda e: e.tensor_tensor(s_ln4[:, 2:3], s_mv[:, 0:1],
                                                          s_mv[:, 0:1], ALU.mult), [tm])
            tbb = emt("scalar", lambda e: e.copy(s_ln4[:, 3:4], s_ln4[:, 2:3]), [tmm])
            tv = emt("vector", lambda e: e.scalar_tensor_tensor(s_mv[:, 1:2], s_ln4[:, 1:2],
                                                                1.0 / 1024.0, s_ln4[:, 2:3],
                                                                ALU.mult, ALU.subtract), [tbb, t2])
            t3 = emt("scalar", lambda e: e.activation(s_rstd[:], s_mv[:, 1:2], AF.Sqrt, bias=1e-5),
                     waits=[tv])
            t4 = emt("vector", lambda e: e.reciprocal(s_rstd[:], s_rstd[:]), waits=[t3])
            t5 = emt("scalar", lambda e: e.copy(s_ln4[:, 3:4], s_rstd[:]), [t4])
            return emt("vector", lambda e: e.tensor_scalar(dst_ap, src_ap, s_mv[:, 0:1],
                                                           s_rstd[:, 0:1], ALU.subtract, ALU.mult),
                       waits=[t5])

        g_qt = [None]

        def emit_quant(x_ap, out_ap, man, waits, tA, tB, tC, tD, bounce=False):
            Cm = kcol(2) if man == 3 else kcol(5)
            Cab = kcol(3)
            minexp = -6 if man == 3 else -14
            invg = float(2.0 ** (man - minexp))
            g = float(2.0 ** (minexp - man))
            thr = float(2.0 ** (2 * minexp))
            w = list(waits) + [setup_tok, g_qt[0]]
            t_w = emt("gpsimd", lambda e: e.tensor_scalar(tA, x_ap, Cm, None, ALU.mult), w)
            t_z = emt("vector", lambda e: e.tensor_tensor(tB, tA, x_ap, ALU.subtract), [t_w])
            if bounce:
                t_z = emt("scalar", lambda e: e.copy(s_ln4[:, 3:4], tB[:, 0:1]), [t_z])
            t_v = emt("vector", lambda e: e.tensor_tensor(tA, tA, tB, ALU.subtract), [t_z])
            t_a = emt("gpsimd", lambda e: e.tensor_scalar(tC, x_ap, invg, None, ALU.mult), w)
            t_a = emt("gpsimd", lambda e: e.tensor_scalar(tC, tC, Cab, Cab, ALU.add, ALU.subtract), [t_a])
            t_a = emt("gpsimd", lambda e: e.tensor_scalar(tC, tC, g, None, ALU.mult), [t_a])
            t_m = emt("scalar", lambda e: e.activation(tD, x_ap, AF.Square), w)
            t_m = emt("gpsimd", lambda e: e.tensor_scalar(tD, tD, thr, None, ALU.is_ge), [t_m])
            if bounce:
                t_v = emt("scalar", lambda e: e.copy(s_ln4[:, 3:4], tA[:, 0:1]), [t_v])
            t_b = emt("vector", lambda e: e.tensor_tensor(tA, tA, tC, ALU.subtract), [t_v, t_a])
            if bounce:
                t_b = emt("scalar", lambda e: e.copy(s_ln4[:, 3:4], tA[:, 0:1]), [t_b])
            t_b = emt("vector", lambda e: e.tensor_tensor(tA, tA, tD, ALU.mult), [t_b, t_m])
            if bounce:
                t_b = emt("scalar", lambda e: e.copy(s_ln4[:, 3:4], tA[:, 0:1]), [t_b])
            ret = emt("vector", lambda e: e.tensor_tensor(out_ap, tA, tC, ALU.add), [t_b])
            g_qt[0] = ret
            return ret

        bank_guard = dict(bank_guard_init)

        def transpose_1024(src_bf, dst3, waits):
            tt = None
            for dc in range(8):
                wl = list(waits) + [init_ld] if dc == 0 else []
                if dc == 0:
                    wl.append(bank_guard.get("PT"))
                tt = emt("tensor", lambda e, dc=dc: e.transpose(
                    PT[:, dc * 128:(dc + 1) * 128], src_bf[:, dc * 128:(dc + 1) * 128],
                    s_eye[:]), waits=wl)
            tev = emt("scalar", lambda e: e.copy(dst3, PT[:].rearrange("p (dc n) -> p dc n", dc=8)),
                      waits=[tt])
            bank_guard["PT"] = tev
            return tev

        v_x = d_x.rearrange("(n p) d -> n p d", p=128)
        v_xq = d_xq.rearrange("(n p) d -> n p d", p=128)
        v_kS = d_kS.rearrange("hp p t -> p hp t")
        v_vS = d_vS.rearrange("(n p) f -> p n f", p=128)
        v_vSr = d_vS.rearrange("(n p) f -> n p f", p=128)
        v_qS = d_qS.rearrange("hp p t -> p hp t")
        v_x2S = d_x2S.rearrange("(n p) d -> n p d", p=128)
        v_x2Sp = d_x2S.rearrange("(n p) d -> p n d", p=128)
        v_w2T = d_w2T.rearrange("(n p) f -> n p f", p=128)
        v_w2qS = d_w2qS.rearrange("(n p) f -> p n f", p=128)
        v_w1T = d_w1T.rearrange("(c p) f -> p c f", p=128)
        v_w1bS = d_w1bS.rearrange("(c p) f -> p c f", p=128)
        v_y = d_y.rearrange("(n p) d -> n p d", p=128)

        PHASES = {"A1": 0, "A2": 1, "B": 2, "C": 3}
        _upto = PHASES[upto]
        g_vfm = [None, None]
        g_xst = [None, None]
        g_lnbf = [None, None]
        g_xln = [None, None]
        g_kst = [None, None]
        g_vst = [None, None]
        g_qst = [None, None]

        def ln_transpose_block(v_src, blk_idx, pb, sd_base, dbg_ln=False):
            xst = a32(FA0 + pb * 4096, 4096)
            xev = None
            tlds = []
            for c in range(4):
                tlds.append(dma("sync", SD[sd_base + pb], xst[:, c * 1024:(c + 1) * 1024],
                                v_src[blk_idx * 4 + c], waits=[g_xst[pb]] if c == 0 else []))
            for c in range(4):
                lnb = a16(A_LNBF + (c % 2) * 1024, 1024)
                tn = emit_ln(xst[:, c * 1024:(c + 1) * 1024], lnb,
                             waits=[tlds[-1], g_lnbf[c % 2], init_done, wload])
                dst3 = a16(A_XLN + pb * 4096, 4096).rearrange("p (dc n) -> p dc n", dc=8)
                xev = transpose_1024(lnb, dst3[:, :, c * 128:(c + 1) * 128],
                                     waits=[tn, g_xln[pb]] if c == 0 else [tn])
                g_lnbf[c % 2] = xev
                if dumps and "a" in _DS and dbg_ln:
                    g_lnbf[c % 2] = dma("sync", cDS, d_lnS[blk_idx * 4 + c], lnb, waits=[tn])
            if dumps and "a" in _DS and dbg_ln:
                dma("sync", cDS, d_xlnS[blk_idx], a16(A_XLN + pb * 4096, 4096), waits=[xev])
            g_xst[pb] = xev
            return xev

        def proj_fm(wofs, xln_off, ntok, nfc, pbanks, dst_off, waits_first):
            evs = []
            for fc in range(nfc):
                bk = pbanks[fc % 2]
                tm = None
                for dc in range(8):
                    wl = []
                    if dc == 0:
                        wl = [bank_guard.get(f"P{bk}")] + list(waits_first)
                    tm = emt("tensor", lambda e, fc=fc, dc=dc, bk=bk: e.matmul(
                        P[bk][:, 0:ntok],
                        a16(wofs + dc * 1024 + fc * 128, 128),
                        a16(xln_off + dc * ntok, ntok),
                        start=(dc == 0), stop=(dc == 7)), waits=wl)
                tev = emt("scalar", lambda e, fc=fc, bk=bk: e.copy(
                    a16(dst_off + fc * ntok, ntok), P[bk][:, 0:ntok]), waits=[tm])
                bank_guard[f"P{bk}"] = tev
                evs.append(tev)
            return evs

        for blkI in range(8):
            pb = blkI % 2
            import os as _os
            t_xln = ln_transpose_block(v_x, blkI, pb, sd_base=0,
                                       dbg_ln=(blkI < 8 and _os.environ.get("DBGLN", "1") == "1"))
            xln_off = A_XLN + pb * 4096
            if a1mode == "ln":
                g_xln[pb] = t_xln
                continue
            evs = proj_fm(WK_O, xln_off, 512, 8, (0, 1), A_KST + pb * 4096,
                          [t_xln, wload, g_kst[pb]])
            tkst = dma("sync", SD[2 + pb], v_kS[:, :, blkI * 512:(blkI + 1) * 512],
                       a16(A_KST + pb * 4096, 4096).rearrange("p (hp t) -> p hp t", hp=8),
                       waits=evs)
            g_kst[pb] = tkst
            evsv = proj_fm(WV_O, xln_off, 512, 8, (2, 3), A_QST + pb * 4096,
                           [t_xln, wload, g_vfm[pb]])
            vfm3 = a16(A_QST + pb * 4096, 4096).rearrange("p (fc t) -> p fc t", fc=8)
            tt = None
            for tc in range(4):
                for fc in range(8):
                    wl = [evsv[-1]]
                    if fc == 0:
                        wl.append(bank_guard.get("PT"))
                        wl.append(g_vst[pb])
                    tt = emt("tensor", lambda e, tc=tc, fc=fc, vfm3=vfm3: e.transpose(
                        PT[:, fc * 128:(fc + 1) * 128], vfm3[:, fc, tc * 128:(tc + 1) * 128],
                        s_eye[:]), waits=wl)
                tev = emt("scalar", lambda e, pb=pb: e.copy(
                    a16(A_VST + pb * 1024, 1024).rearrange("p (fc n) -> p fc n", fc=8),
                    PT[:].rearrange("p (fc n) -> p fc n", fc=8)), waits=[tt])
                bank_guard["PT"] = tev
                tvst = dma("sync", SD[4 + pb], v_vSr[blkI * 4 + tc],
                           a16(A_VST + pb * 1024, 1024), waits=[tev])
                g_vst[pb] = tvst
            g_xln[pb] = evsv[-1]
            g_vfm[pb] = tt
        for qb in range(4 if _upto >= 1 else 0):
            pb = qb % 2
            t_xln = ln_transpose_block(v_xq, qb, pb, sd_base=0)
            xln_off = A_XLN + pb * 4096
            evs = proj_fm(WQ_O, xln_off, 512, 8, (0, 1), A_QST + pb * 4096,
                          [t_xln, wload, g_qst[pb]])
            tq = dma("sync", SD[6 + pb], v_qS[:, :, qb * 512:(qb + 1) * 512],
                     a16(A_QST + pb * 4096, 4096).rearrange("p (hp t) -> p hp t", hp=8),
                     waits=evs)
            g_qst[pb] = tq
            g_xln[pb] = evs[-1]

        barrier()

        w2_chunk_idx = [0]
        g_w2st = [None, None]

        def emit_w2_chunk():
            wc = w2_chunk_idx[0]
            if wc >= 32:
                return
            w2_chunk_idx[0] += 1
            pb = wc % 2
            st = a32(FA0 + 6144 + pb * 1024, 1024)
            stb = a16(WQ_O + pb * 1024, 1024)
            tld = dma("sync", SD[2 + pb], st, v_w2T[wc], waits=[g_w2st[pb]])
            tq = emit_quant(st, stb, 3, [tld], a32(qt[0], 1024), a32(qt[1], 1024),
                            a32(qt[2], 1024), a32(qt[3], 1024))
            tst = dma("sync", SD[6], v_w2qS[:, wc, :], stb, waits=[tq])
            g_w2st[pb] = tst

        g_kc = [None, None]
        g_e = {}
        g_vall = None
        g_qblk = None
        g_afm = None
        g_xqst = [None, None]
        g_x2st = [None, None]
        g_epi = []
        g_njunk = [None]

        for Q in range(4 if _upto >= 2 else 0):
            L = 8 * Q + 8
            t_vall = []
            for gviii in range(0, L, 8):
                nL = min(8, L - gviii)
                t_vall.append(dma("sync", SD[9],
                                  a16(B_VAL + gviii * 1024, nL * 1024).rearrange(
                                      "p (n f) -> p n f", n=nL),
                                  v_vS[:, gviii:gviii + nL, :], waits=[g_vall]))
            t_qblk = dma("sync", SD[7], a16(B_QB, 4096).rearrange("p (hp t) -> p hp t", hp=8),
                         v_qS[:, :, Q * 512:(Q + 1) * 512], waits=[g_qblk])
            for hp in range(8):
                pbk = hp % 2
                t_kc = dma("sync", SD[pbk], a16(B_KC + pbk * 4096, L * 128),
                           v_kS[:, hp, 0:L * 128], waits=[g_kc[pbk]])
                t_scores_last = None
                t_attnv = [None, None]
                t_S = [None, None]
                for j in range(L):
                    nq = max(0, (j - 8 * Q) // 2)
                    off = nq * 128
                    sa, sb_ = (0, 1) if j % 2 == 0 else (2, 3)
                    ktile = a16(B_KC + pbk * 4096 + j * 128, 128)
                    wl_a = [t_kc, t_qblk, bank_guard.get(f"P{sa}")]
                    wl_b = [bank_guard.get(f"P{sb_}")]
                    tsc_a = emt("tensor", lambda e, sa=sa, off=off, hp=hp, kt=ktile: e.matmul(
                        P[sa][:, off:512], kt[0:64, :],
                        a16(B_QB + hp * 512 + off, 512 - off)[0:64, :],
                        start=True, stop=True), waits=wl_a)
                    tsc_b = emt("tensor", lambda e, sb_=sb_, off=off, hp=hp, kt=ktile: e.matmul(
                        P[sb_][:, off:512], kt[64:128, :],
                        a16(B_QB + hp * 512 + off, 512 - off)[64:128, :],
                        start=True, stop=True, tile_position=(64, 0)), waits=wl_b)
                    tm_a, tm_b = tsc_a, tsc_b
                    if j >= 8 * Q:
                        msk = s_md2 if (j - 8 * Q) % 2 == 0 else s_md1
                        tm_a = emt("vector", lambda e, sa=sa, off=off, m=msk: e.tensor_tensor(
                            P[sa][:, off:off + 128], P[sa][:, off:off + 128],
                            m[:], ALU.add), waits=[tsc_a, init_ld])
                        tm_b = emt("vector", lambda e, sb_=sb_, off=off, m=msk: e.tensor_tensor(
                            P[sb_][:, off:off + 128], P[sb_][:, off:off + 128],
                            m[:], ALU.add), waits=[tsc_b, init_ld])
                    pe = j % 2
                    e_a = a16(B_E + pe * 512, 512)
                    e_b = a16(B_E + 1024 + pe * 512, 512)
                    tex_a = emt("scalar", lambda e, sa=sa, off=off, ea=e_a: e.activation(
                        ea[:, off:512], P[sa][:, off:512], AF.Exp, scale=1.0 / 64.0),
                        waits=[tm_a, g_e.get(("a", pe))])
                    tex_b = emt("scalar", lambda e, sb_=sb_, off=off, eb=e_b: e.activation(
                        eb[:, off:512], P[sb_][:, off:512], AF.Exp, scale=1.0 / 64.0),
                        waits=[tm_b, g_e.get(("b", pe))])
                    bank_guard[f"P{sa}"] = tex_a
                    bank_guard[f"P{sb_}"] = tex_b
                    vt = a16(B_VAL + j * 1024 + hp * 128, 128)
                    wl_o = [tex_a] + t_vall
                    if j == 0:
                        wl_o.append(bank_guard.get(f"P{4 + pbk}"))
                    ta = emt("tensor", lambda e, pbk=pbk, off=off, vt=vt, ea=e_a, j=j, L=L: e.matmul(
                        P[4 + pbk][0:64, off:512], vt[:, 0:64], ea[:, off:512],
                        start=(j == 0), stop=(j == L - 1), tile_position=(0, 0),
                        skip_group_check=True), waits=wl_o)
                    wl_s = [tex_a]
                    if j == 0:
                        wl_s.append(bank_guard.get("P6"))
                    tSa = emt("tensor", lambda e, off=off, ea=e_a, j=j, L=L: e.matmul(
                        P[6][0:1, off:512], s_ones[:, 0:1], ea[:, off:512],
                        start=(j == 0), stop=(j == L - 1), skip_group_check=True), waits=wl_s)
                    tb = emt("tensor", lambda e, pbk=pbk, off=off, vt=vt, eb=e_b, j=j, L=L: e.matmul(
                        P[4 + pbk][64:128, off:512], vt[:, 64:128], eb[:, off:512],
                        start=(j == 0), stop=(j == L - 1), tile_position=(0, 64),
                        skip_group_check=True), waits=[tex_b])
                    tSb = emt("tensor", lambda e, off=off, eb=e_b, j=j, L=L: e.matmul(
                        P[6][32:33, off:512], s_ones[:, 1:2], eb[:, off:512],
                        start=(j == 0), stop=(j == L - 1), tile_position=(0, 32),
                        skip_group_check=True), waits=[tex_b])
                    t_attnv = [ta, tb]
                    t_S = [tSa, tSb]
                    g_e[("a", pe)] = tSa
                    g_e[("b", pe)] = tSb
                    if dumps and "b" in _DS and Q == 0 and j < 2:
                        g_e[("a", pe)] = dma("sync", cDS, d_eS[hp][j * 2], e_a,
                                             waits=[tSa, tb])
                        g_e[("b", pe)] = dma("sync", cDS, d_eS[hp][j * 2 + 1], e_b,
                                             waits=[tSb])
                    t_scores_last = (tsc_a, tsc_b)
                g_kc[pbk] = t_scores_last[1]
                tr_a = emt("vector", lambda e: e.reciprocal(s_Bb[0:1, :], P[6][0:1, :]),
                           waits=t_S + t_attnv)
                tr_b = emt("vector", lambda e: e.reciprocal(s_Bb[32:33, :], P[6][32:33, :]),
                           waits=[])
                bank_guard["P6"] = tr_b
                tr_a = emt("scalar", lambda e: e.copy(s_rS[0:1, :], s_Bb[0:1, :]), waits=[tr_a])
                tr_b = emt("scalar", lambda e: e.copy(s_rS[32:33, :], s_Bb[32:33, :]), waits=[tr_b])
                wl_B = [tr_a, tr_b, bank_guard.get("P0")]
                tB1 = emt("tensor", lambda e: e.matmul(P[0][0:64, :], s_ones[0:1, 0:64],
                                                       s_rS[0:1, :], start=True, stop=True,
                                                       skip_group_check=True), waits=wl_B)
                tB2 = emt("tensor", lambda e: e.matmul(P[0][64:128, :], s_ones[32:33, 0:64],
                                                       s_rS[32:33, :], start=True, stop=True,
                                                       tile_position=(32, 64),
                                                       skip_group_check=True), waits=[])
                tBe = emt("scalar", lambda e: e.copy(s_Bb[:], P[0][:]), waits=[tB1, tB2])
                bank_guard["P0"] = tBe
                if dumps and "b" in _DS and Q == 0:
                    tBe = dma("sync", cDS, d_sbbS[hp], s_Bb[:], waits=[tBe])
                    bank_guard["P0"] = tBe
                tno = emt("vector", lambda e, hp=hp, pbk=pbk: e.tensor_tensor(
                    a16(B_AFM + hp * 512, 512), P[4 + pbk][:], s_Bb[:], ALU.mult),
                    waits=[tBe, g_afm] + t_attnv)
                bank_guard[f"P{4 + pbk}"] = tno
                if dumps and "b" in _DS and Q == 0:
                    t_nc = emt("scalar", lambda e, pbk=pbk: e.copy(
                        s_junk[:, 0:512], P[4 + pbk][:]), waits=[tno, g_njunk[0]])
                    g_njunk[0] = dma("sync", cDS, d_numS[hp], s_junk[:, 0:512],
                                     waits=[t_nc])
                    bank_guard[f"P{4 + pbk}"] = g_njunk[0]
                if hp == 0:
                    afm_toks = []
                afm_toks.append(tno)
                emit_w2_chunk()
            t_afm_dump = None
            if dumps and "b" in _DS:
                t_afm_dump = dma("sync", cDS, d_afmS[Q], a16(B_AFM, 4096), waits=afm_toks)
            wo_last = None
            for tc in range(4):
                ptc = tc % 2
                xqst = a32(FA0 + ptc * 1024, 1024)
                t_xq = dma("sync", SD[4 + ptc], xqst, v_xq[Q * 4 + tc], waits=[g_xqst[ptc]])
                tW = [None, None]
                for nh in range(2):
                    bk = 4 + nh
                    tm = None
                    for dc in range(8):
                        wl = afm_toks + [bank_guard.get(f"P{bk}")] if dc == 0 else []
                        tm = emt("tensor", lambda e, tc=tc, nh=nh, dc=dc, bk=bk: e.matmul(
                            P[bk][:], a16(B_AFM + dc * 512 + tc * 128, 128),
                            a16(WO_O + dc * 1024 + nh * 512, 512),
                            start=(dc == 0), stop=(dc == 7)), waits=wl)
                    tW[nh] = tm
                    wo_last = tm
                o1 = a32(FA0 + 2048, 1024)
                to1a = emt("vector", lambda e, o1=o1: e.tensor_tensor(
                    o1[:, 0:512], P[4][:], s_bo[:, 0:512], ALU.add), waits=[tW[0], t_bo])
                to1b = emt("vector", lambda e, o1=o1: e.tensor_tensor(
                    o1[:, 512:1024], P[5][:], s_bo[:, 512:1024], ALU.add), waits=[tW[1]])
                bank_guard["P4"] = to1a
                bank_guard["P5"] = to1b
                if dumps and "b" in _DS:
                    t_do1 = dma("sync", cDS, d_epiS[Q * 4 + tc][1], o1,
                                waits=[to1a, to1b])
                    bank_guard["P4"] = t_do1
                o2 = a32(FA0 + 3072, 1024)
                tln = emit_ln(o1, o2, waits=[to1a, to1b] + g_epi)
                x2 = a32(FA0 + 4096 + ptc * 1024, 1024)
                tx2 = emt("vector", lambda e, x2=x2, xqst=xqst, o2=o2: e.tensor_tensor(
                    x2, xqst, o2, ALU.add), waits=[tln, t_xq, g_x2st[ptc]])
                g_xqst[ptc] = tx2
                if dumps and "b" in _DS:
                    t_dxq = dma("sync", cDS, d_epiS[Q * 4 + tc][0], xqst, waits=[tx2])
                    t_do2 = dma("sync", cDS, d_epiS[Q * 4 + tc][2], o2, waits=[tx2])
                    g_xqst[ptc] = t_dxq
                    g_epi = [t_do2]
                tst = dma("sync", cDS, v_x2S[Q * 4 + tc], x2, waits=[tx2])
                g_x2st[ptc] = tst
            g_vall = t_attnv[0]
            g_qblk = t_scores_last[0]
            g_afm = wo_last if t_afm_dump is None else t_afm_dump
        while _upto >= 2 and w2_chunk_idx[0] < 32:
            emit_w2_chunk()

        barrier()
        bank_guard.clear()

        t_w2q = []
        for i in range(4 if _upto >= 3 else 0):
            t_w2q.append(dma("sync", SD[i], a16(C_W2Q + i * 8192, 8192).rearrange(
                "p (n f) -> p n f", n=8), v_w2qS[:, i * 8:(i + 1) * 8, :]))
        g_w1st = [None, None]
        t_w1c = []
        for c8 in range(8 if _upto >= 3 else 0):
            pb = c8 % 2
            stg = a16(C_W1G + pb * 4096, 4096)
            tcs = []
            for hf in range(2):
                stf = a32(FA0 + (2 * c8 + hf) % 2 * 2048, 2048)
                tin = dma("sync", SD[8 + (2 * c8 + hf) % 2], stf,
                          v_w1T[:, c8, hf * 2048:(hf + 1) * 2048], waits=[g_w1st[(2 * c8 + hf) % 2]])
                tc_ = emt("scalar", lambda e, stg=stg, stf=stf, hf=hf: e.copy(
                    stg[:, hf * 2048:(hf + 1) * 2048], stf), waits=[tin])
                g_w1st[(2 * c8 + hf) % 2] = tc_
                tcs.append(tc_)
            tout = dma("sync", cDS, v_w1bS[:, c8, :], stg, waits=tcs)
            t_w1c.append(tout)
        tb2q = None
        if _upto >= 3:
            tb2q = emit_quant(s_b2[:], s_b2q[:], 10, [init_ld], a32(qt[0], 8), a32(qt[1], 8),
                              a32(qt[2], 8), a32(qt[3], 8), bounce=True)

        g_x2c = [t_w1c[-1] if t_w1c else None, t_w1c[-1] if t_w1c else None]
        g_w1g = [None, None]
        g_hq = [None] * 8
        g_hqd = [None] * 8
        g_h = [None, None]
        g_yst = [None, None]
        g_xln2 = []
        g_lnb = [[], []]
        g_x2cd = [None, None]
        for T in range(8 if _upto >= 3 else 0):
            pT = T % 2
            x2c = a32(FA0 + pT * 2048, 2048)
            t_x2 = dma("sync", SD[4 + pT], x2c.rearrange("p (n f) -> p n f", n=2),
                       v_x2Sp[:, T * 2:T * 2 + 2, :], waits=[g_x2c[pT], g_x2cd[pT]])
            xev = None
            xevs = []
            if dumps and "c1" in _DS:
                g_x2cd[pT] = dma("sync", cDS, d_x2cS[T], x2c, waits=[t_x2])
            for tc2 in range(2):
                lnb = a16(C_XLN2 + 2048 + (tc2 % 2) * 1024, 1024)
                tn = emit_ln(x2c[:, tc2 * 1024:(tc2 + 1) * 1024], lnb,
                             waits=[t_x2] + g_lnb[tc2])
                if dumps and "c1" in _DS:
                    tn = dma("sync", cDS, d_lnbS[T][tc2], lnb, waits=[tn])
                dst3 = a16(C_XLN2, 2048).rearrange("p (dc n) -> p dc n", dc=8)
                xev = transpose_1024(lnb, dst3[:, :, tc2 * 128:(tc2 + 1) * 128],
                                     waits=[tn] + g_xln2)
                g_lnb[tc2] = [xev]
                xevs.append(xev)
            if dumps and "c1" in _DS:
                g_xln2 = [dma("sync", cDS, d_xln2S[T], a16(C_XLN2, 2048), waits=xevs)]
            ffn2_last = None
            t_hqs = []
            lastA = [None] * 4
            for hg in range(8):
                phg = hg % 2
                w1g = a16(C_W1G + phg * 4096, 4096)
                t_w1 = dma("sync", SD[6 + phg], w1g.rearrange("p (c f) -> p c f", c=8),
                           v_w1bS[:, :, hg * 512:(hg + 1) * 512],
                           waits=[g_w1g[phg]] + (t_w1c if T == 0 and hg < 2 else []))
                hsb = a32(FA0 + 4096 + phg * 1024, 1024)
                t_h = []
                for hc in range(4):
                    bk = 4 + hc % 2
                    tm = None
                    for dc in range(8):
                        wl = []
                        if dc == 0:
                            wl = [bank_guard.get(f"P{bk}"), t_w1, xev]
                        tm = emt("tensor", lambda e, hc=hc, dc=dc, bk=bk, phg=phg: e.matmul(
                            P[bk][:, 0:256], a16(C_W1G + phg * 4096 + dc * 512 + hc * 128, 128),
                            a16(C_XLN2 + dc * 256, 256), start=(dc == 0), stop=(dc == 7)),
                            waits=wl)
                    trl = emt("scalar", lambda e, hc=hc, bk=bk, hg=hg, hsb=hsb: e.activation(
                        hsb[:, hc * 256:(hc + 1) * 256], P[bk][:, 0:256], AF.Relu,
                        bias=s_b1[:, hg * 4 + hc:hg * 4 + hc + 1]),
                        waits=[tm, init_ld, g_h[phg] if hc == 0 else None])
                    bank_guard[f"P{bk}"] = trl
                    t_h.append(trl)
                hq = a16(C_HQ + hg * 1024, 1024)
                t_hq = emit_quant(hsb, hq, 3, t_h + [g_hq[hg], g_hqd[hg]],
                                  a32(qt[0], 1024), a32(qt[1], 1024), a32(qt[2], 1024),
                                  a32(qt[3], 1024))
                g_h[phg] = t_hq
                t_hqs.append(t_hq)
                if dumps and "c2" in _DS:
                    g_hqd[hg] = dma("sync", cDS, d_hqS[T][hg], hq, waits=[t_hq])
                for hc in range(4):
                    for fc in range(4):
                        wl = [t_hq] + t_w2q
                        if hg == 0 and hc == 0:
                            wl.append(bank_guard.get(f"PF{fc}"))
                        tfa = emt("tensor", lambda e, hc=hc, fc=fc, hg=hg: e.matmul(
                            P[fc][:, 0:256],
                            a16(C_W2Q + (hg * 4 + hc) * 1024 + fc * 128, 128),
                            a16(C_HQ + hg * 1024 + hc * 256, 256),
                            start=(hg == 0 and hc == 0),
                            stop=(hg == 7 and hc == 3), skip_group_check=True), waits=wl)
                        lastA[fc] = tfa
                ffn1_last = tm
                g_w1g[phg] = tm
            o_fm = a32(FA0 + 6144, 2048)
            t_ev = []
            for fc in range(4):
                tev_ = emt("vector", lambda e, fc=fc, o_fm=o_fm: e.tensor_scalar(
                    o_fm[:, fc * 256:(fc + 1) * 256], P[fc][:, 0:256],
                    s_b2q[:, fc:fc + 1], None, ALU.add),
                    waits=[lastA[fc], tb2q, g_yst[0], g_yst[1]])
                bank_guard[f"PF{fc}"] = tev_
                t_ev.append(tev_)
            lastB = [None] * 4
            for hg in range(8):
                for hc in range(4):
                    for fc in range(4, 8):
                        wl = []
                        if hg == 0 and hc == 0:
                            wl = [bank_guard.get(f"PF{fc - 4}"), t_hqs[-1]]
                        tfb = emt("tensor", lambda e, hc=hc, fc=fc, hg=hg: e.matmul(
                            P[fc - 4][:, 0:256],
                            a16(C_W2Q + (hg * 4 + hc) * 1024 + fc * 128, 128),
                            a16(C_HQ + hg * 1024 + hc * 256, 256),
                            start=(hg == 0 and hc == 0),
                            stop=(hg == 7 and hc == 3), skip_group_check=True), waits=wl)
                        lastB[fc - 4] = tfb
                g_hq[hg] = lastB[3]
            ffn2_last = lastB[3]
            for fc in range(4, 8):
                tev_ = emt("vector", lambda e, fc=fc, o_fm=o_fm: e.tensor_scalar(
                    o_fm[:, fc * 256:(fc + 1) * 256], P[fc - 4][:, 0:256],
                    s_b2q[:, fc:fc + 1], None, ALU.add),
                    waits=[lastB[fc - 4], g_yst[0], g_yst[1]])
                bank_guard[f"PF{fc - 4}"] = tev_
                t_ev.append(tev_)
            if dumps and "c3" in _DS:
                t_ev = t_ev + [dma("sync", cDS, d_ofmS[T][0], o_fm, waits=t_ev)]
            tq1 = emit_quant(o_fm[:, 0:1024], o_fm[:, 0:1024], 10, t_ev, a32(qt[0], 1024),
                             a32(qt[1], 1024), a32(qt[2], 1024), a32(qt[3], 1024))
            tq2 = emit_quant(o_fm[:, 1024:2048], o_fm[:, 1024:2048], 10, t_ev, a32(qt[0], 1024),
                             a32(qt[1], 1024), a32(qt[2], 1024), a32(qt[3], 1024))
            if dumps and "c3" in _DS:
                tq2 = dma("sync", cDS, d_ofmS[T][1], o_fm, waits=[tq1, tq2])
            o_tm = a32(FA0 + 4096, 2048)
            for tc2 in range(2):
                for fcg in range(2):
                    tt = None
                    for i in range(4):
                        fc = fcg * 4 + i
                        wl = [tq1, tq2, bank_guard.get("P6"), init_ld] if i == 0 else []
                        tt = emt("tensor", lambda e, fc=fc, tc2=tc2, i=i, o_fm=o_fm: e.transpose(
                            P[6][:, i * 128:(i + 1) * 128],
                            o_fm[:, fc * 256 + tc2 * 128: fc * 256 + tc2 * 128 + 128],
                            s_eyef[:]), waits=wl)
                    tev = emt("vector", lambda e, tc2=tc2, fcg=fcg, o_tm=o_tm: e.tensor_copy(
                        o_tm[:, tc2 * 1024 + fcg * 512: tc2 * 1024 + fcg * 512 + 512],
                        P[6][:]), waits=[tt])
                    bank_guard["P6"] = tev
            g_otmd = None
            if dumps and "c3" in _DS:
                g_otmd = dma("sync", cDS, d_otmS[T], o_tm, waits=[bank_guard["P6"]])
            o3 = a32(FA0 + 6144, 1024)
            yst = a32(FA0 + 7168, 1024)
            tlns = []
            t_prev_y = None
            for tc2 in range(2):
                tln = emit_ln(o_tm[:, tc2 * 1024:(tc2 + 1) * 1024], o3,
                              waits=[bank_guard["P6"], g_yst[tc2], g_otmd])
                tlns.append(tln)
                tx3 = emt("vector", lambda e, tc2=tc2, x2c=x2c, o3=o3, yst=yst: e.tensor_tensor(
                    yst, x2c[:, tc2 * 1024:(tc2 + 1) * 1024], o3, ALU.add),
                    waits=[tln, t_prev_y])
                tyD = dma("sync", SD[8 + tc2], v_y[T * 2 + tc2], yst, waits=[tx3])
                t_prev_y = tyD
                g_yst[tc2] = tyD
            g_x2c[pT] = g_yst[1]
            g_w1g = [ffn2_last, ffn2_last]
            g_h = [tlns[0], tlns[1]]
            g_xln2 = (g_xln2 if dumps else []) + [ffn1_last]

        barrier()

        import os as _os
        DEDUP = _os.environ.get("NODEDUP", "0") != "1"
        last_wait = {}

        def body(engname):
            def run(eng):
                for (en, fn, waits, inc) in order:
                    if en != engname:
                        continue
                    for (ctr, v) in waits:
                        if ctr is ENG_CTR.get(engname) and engname != "gpsimd":
                            continue
                        key = (engname, id(ctr))
                        if DEDUP and last_wait.get(key, -1) >= v:
                            continue
                        eng.wait_ge(ctr.sem, v * ctr.step)
                        last_wait[key] = v
                    if fn is not None:
                        instr = fn(eng)
                        if inc is not None:
                            instr.then_inc(inc.sem, inc.step)
            return run

        blk = es.enter_context(nc.Block())
        blk.sync(body("sync"))
        blk.tensor(body("tensor"))
        blk.vector(body("vector"))
        blk.scalar(body("scalar"))
        blk.gpsimd(body("gpsimd"))
    return nc


_NC = None


def kernel(L0_attn_x, L0_attn_Wq, L0_attn_Wk, L0_attn_Wv, L0_attn_Wo, L0_attn_bo,
           L0_ffn_w1, L0_ffn_b1, L0_ffn_w2, L0_ffn_b2):
    global _NC
    x = np.ascontiguousarray(np.asarray(L0_attn_x, dtype=np.float32))
    wqT = np.ascontiguousarray(np.asarray(L0_attn_Wq, np.float32).T)
    wkT = np.ascontiguousarray(np.asarray(L0_attn_Wk, np.float32).T)
    wvT = np.ascontiguousarray(np.asarray(L0_attn_Wv, np.float32).T)
    woT = np.ascontiguousarray(np.asarray(L0_attn_Wo, np.float32).T)
    w1T = np.ascontiguousarray(np.asarray(L0_ffn_w1, np.float32).T)
    w2T = np.ascontiguousarray(np.asarray(L0_ffn_w2, np.float32).T)
    b1 = np.ascontiguousarray(np.asarray(L0_ffn_b1, np.float32).reshape(32, 128).T)
    b2 = np.ascontiguousarray(np.asarray(L0_ffn_b2, np.float32).reshape(8, 128).T)
    bo = np.ascontiguousarray(np.asarray(L0_attn_bo, np.float32))

    eye = np.eye(128, dtype=ml_dtypes.bfloat16)
    eyef = np.eye(128, dtype=np.float32)
    kk, qq = np.meshgrid(np.arange(128), np.arange(128), indexing="ij")
    tri = np.where(kk > qq, np.float32(-1.0e30), np.float32(0.0))
    zer = np.zeros((128, 128), np.float32)
    ful = np.full((128, 128), np.float32(-1.0e30))

    if _NC is None:
        _NC = build_nc()

    in_maps = []
    slot_rows = {}
    for c in range(8):
        b, h = c // 2, c % 2
        own = [2 * s + (1 - h) for s in range(16)]
        rows = np.concatenate([np.arange(t * 128, (t + 1) * 128) for t in own])
        slot_rows[c] = rows
        xq = np.ascontiguousarray(x[b][rows])
        md1, md2 = (tri, zer) if h == 0 else (ful, tri)
        in_maps.append({
            "x": np.ascontiguousarray(x[b]), "xq": xq,
            "wqT": wqT, "wkT": wkT, "wvT": wvT, "woT": woT,
            "w1T": w1T, "w2T": w2T, "b1": b1, "b2": b2, "bo": bo,
            "eye": eye, "eyef": eyef, "md1": md1, "md2": md2,
        })

    res = run_bass_kernel_spmd(_NC, in_maps, core_ids=list(range(8)))
    out = np.empty((B, S, D), dtype=np.float32)
    for c in range(8):
        b = c // 2
        out[b][slot_rows[c]] = res.results[c]["y"]
    return out
